# version 13
# baseline (speedup 1.0000x reference)
import math
import contextlib
import numpy as np
import ml_dtypes
import concourse.bass as bass
import concourse.mybir as mybir
from concourse.bass_utils import run_bass_kernel_spmd

F32 = mybir.dt.float32
BF16 = mybir.dt.bfloat16
AF = mybir.ActivationFunctionType
ALU = mybir.AluOpType
AX = mybir.AxisListType
BFDT = ml_dtypes.bfloat16

NCORES = 8
T = 2048
NT = 16
KC = 8
NEG = -1.0e30
EPS_LN = 1.0e-5

_STAGE = 6
_DBG = None


def _fmm(o, l, r, st, sp, tp=None):
    if tp is None:
        return lambda E: E.matmul(o, l, r, start=st, stop=sp, skip_group_check=True)
    return lambda E: E.matmul(o, l, r, start=st, stop=sp, tile_position=tp,
                              skip_group_check=True)


def _ftr(o, i, idn):
    return lambda E: E.transpose(o, i, idn)


def _fcp(o, i):
    return lambda E: E.tensor_copy(o, i)


def _ftt(o, a, b, op):
    return lambda E: E.tensor_tensor(o, a, b, op)


def _fts(o, a, s1, op0, s2=None, op1=None):
    if s2 is None:
        return lambda E: E.tensor_scalar(o, a, s1, None, op0)
    return lambda E: E.tensor_scalar(o, a, s1, s2, op0, op1)


def _fstt(o, i0, sc, i1, op0, op1):
    return lambda E: E.scalar_tensor_tensor(o, i0, sc, i1, op0, op1)


def _fact(o, i, fn, bias=0.0, scale=1.0, acc=None):
    if acc is None:
        return lambda E: E.activation(o, i, fn, bias=bias, scale=scale)
    return lambda E: E.activation(o, i, fn, bias=bias, scale=scale, accum_out=acc)


def _fred(o, i, op, neg=False):
    return lambda E: E.tensor_reduce(o, i, AX.X, op, negate=neg)


def _frecip(o, i):
    return lambda E: E.reciprocal(o, i)


def _fdma(o, i):
    return lambda E: E.dma_start(o, i)


class _Sem:
    def __init__(self, h, name):
        self.h = h
        self.name = name
        self.n = 0


class _P:
    ENG = ("tensor", "vector", "scalar", "gpsimd", "sync")

    def __init__(self):
        self.q = {e: [] for e in self.ENG}
        self.w = {e: {} for e in self.ENG}

    def on(self, e, f):
        self.q[e].append(f)

    def wait(self, e, sem, v):
        if sem is None or v is None or v <= 0:
            return
        assert v <= sem.n, ("wait on future count", e, sem.name, v, sem.n)
        if self.w[e].get(sem.name, 0) >= v:
            return
        self.w[e][sem.name] = v
        self.on(e, lambda E, s=sem.h, vv=v: E.wait_ge(s, vv))

    def waits(self, e, evts):
        for ev in evts:
            if ev is not None:
                self.wait(e, ev[0], ev[1])

    def do(self, e, f, sem=None, k=1):
        if e in ("vector", "scalar"):
            self.on(e, f)
            if sem is None:
                self.on(e, lambda E: E.drain())
                return None
            sem.n += k
            n = sem.n
            self.on(e, lambda E, s=sem.h, kk=k: E.drain().then_inc(s, kk))
            return (sem, n)
        if sem is None:
            self.on(e, f)
            return None
        sem.n += k
        n = sem.n
        self.on(e, lambda E, ff=f, s=sem.h, kk=k: ff(E).then_inc(s, kk))
        return (sem, n)

    def dob(self, e, fns, sem=None, k=1):
        for f in fns:
            self.on(e, f)
        if sem is None:
            self.on(e, lambda E: E.drain())
            return None
        sem.n += k
        n = sem.n
        self.on(e, lambda E, s=sem.h, kk=k: E.drain().then_inc(s, kk))
        return (sem, n)

    def flush(self, blk):
        for e in self.ENG:
            fns = self.q[e]
            if fns:
                getattr(blk, e)(lambda E, fns=fns: [f(E) for f in fns])


class _Banks:
    def __init__(self, p, tiles):
        self.p = p
        self.t = tiles
        self.free = [None] * len(tiles)

    def acq(self, eng, i):
        if self.free[i] is not None:
            self.p.wait(eng, *self.free[i])
        return self.t[i]

    def rel(self, i, evt):
        self.free[i] = evt


def _blk_lhsT(w, K, M):
    kc, oc = K // 128, M // 128
    a = w.reshape(kc, 128, oc, 128).transpose(2, 1, 0, 3).reshape(oc, 128, kc * 128)
    return np.ascontiguousarray(a)


def _pe_table(Tlen, d):
    pos = np.arange(Tlen, dtype=np.float64)[:, None]
    inv = np.exp(-math.log(10000.0) * np.arange(0, d, 2, dtype=np.float64) / d)
    ang = pos * inv[None, :]
    pe = np.zeros((Tlen, d), dtype=np.float64)
    pe[:, 0::2] = np.sin(ang)
    pe[:, 1::2] = np.cos(ang)
    return pe.astype(np.float32)


def _prep(inp):
    f = np.float32
    h = {}
    stk = []
    for nm in ("L0_attn_Wq", "L0_attn_Wk", "L0_attn_Wv"):
        stk.append(np.asarray(inp[nm], f).reshape(8, 128, 8, 128))
    wqkv0 = np.stack(stk, axis=0).transpose(3, 2, 0, 1, 4).reshape(8, 128, 3 * 8 * 128)
    h["wqkv0"] = np.ascontiguousarray(wqkv0)
    h["wo0"] = np.ascontiguousarray(
        np.asarray(inp["L0_attn_Wo"], f).reshape(8, 128, 1024).transpose(1, 0, 2).reshape(128, 8192))
    h["bq0"] = np.ascontiguousarray(np.asarray(inp["L0_attn_bq"], f).reshape(8, 128).T)
    h["bk0"] = np.ascontiguousarray(np.asarray(inp["L0_attn_bk"], f).reshape(8, 128).T)
    h["bv0"] = np.ascontiguousarray(np.asarray(inp["L0_attn_bv"], f).reshape(8, 128).T)
    h["bo0"] = np.ascontiguousarray(np.asarray(inp["L0_attn_bo"], f).reshape(1, 1024))
    g9 = np.concatenate([np.asarray(inp["L0_ffn_gate_w"], f).T,
                         np.asarray(inp["L0_ffn_shared_expert_gate_w"], f).T], axis=1)
    h["gate9T"] = np.ascontiguousarray(g9.reshape(8, 128, 9).transpose(1, 0, 2).reshape(128, 72))
    egu = np.asarray(inp["L0_ffn_experts_gate_up"], f)
    edn = np.asarray(inp["L0_ffn_experts_down"], f)
    sgw = np.asarray(inp["L0_ffn_shared_gate_w"], f)
    suw = np.asarray(inp["L0_ffn_shared_up_w"], f)
    sdw = np.asarray(inp["L0_ffn_shared_down_w"], f)
    g_list, u_list, d_list = [], [], []
    for e in range(8):
        g_list.append(_blk_lhsT(egu[e][:1408].T, 1024, 1408))
        u_list.append(_blk_lhsT(egu[e][1408:].T, 1024, 1408))
        d_list.append(edn[e].T)
    for sh in range(2):
        rows = slice(sh * 1408, (sh + 1) * 1408)
        g_list.append(_blk_lhsT(sgw[rows].T, 1024, 1408))
        u_list.append(_blk_lhsT(suw[rows].T, 1024, 1408))
        d_list.append(sdw.T[rows])
    gb = np.stack(g_list)
    ub = np.stack(u_list)
    h["egu_blk"] = np.ascontiguousarray(
        np.stack([gb, ub], axis=3).reshape(10, 11, 128, 2048))
    dT = np.stack(d_list)
    h["down_blk"] = np.ascontiguousarray(
        dT.reshape(10, 11, 128, 2, 512).transpose(0, 3, 2, 1, 4).reshape(10, 2, 128, 11 * 512))
    wq1 = np.asarray(inp["L1_attn_Wq"], f)
    h["wq1_blk"] = np.ascontiguousarray(
        wq1.reshape(8, 128, 4, 256).transpose(2, 1, 0, 3).reshape(4, 128, 8 * 256))
    h["wkv1"] = np.ascontiguousarray(
        np.asarray(inp["L1_attn_Wkv"], f).reshape(8, 128, 256).transpose(1, 0, 2).reshape(128, 2048))
    h["wo1"] = np.ascontiguousarray(
        np.asarray(inp["L1_attn_Wo"], f).reshape(8, 128, 1024).transpose(1, 0, 2).reshape(128, 8192))
    h["bq1"] = np.ascontiguousarray(np.asarray(inp["L1_attn_bq"], f).reshape(8, 128).T)
    h["bkv1"] = np.ascontiguousarray(np.asarray(inp["L1_attn_bkv"], f).reshape(2, 128).T)
    h["wgredT"] = np.ascontiguousarray(
        np.asarray(inp["L1_ffn_wg_red"], f).T.reshape(8, 128, 16).transpose(1, 0, 2).reshape(128, 128))
    h["wg"] = np.ascontiguousarray(np.asarray(inp["L1_ffn_wg"], f))
    w1 = np.asarray(inp["L1_ffn_w1"], f)
    w2 = np.asarray(inp["L1_ffn_w2"], f)
    w1b = np.stack([_blk_lhsT(w1[e], 1024, 4096) for e in range(8)])
    h["w1_blk"] = np.ascontiguousarray(
        w1b.reshape(8, 16, 2, 128, 1024).transpose(0, 1, 3, 2, 4).reshape(8, 16, 128, 2048))
    w2b = np.stack([_blk_lhsT(w2[e], 4096, 1024) for e in range(8)])
    h["w2_blk"] = np.ascontiguousarray(w2b)
    h["b1"] = np.ascontiguousarray(
        np.asarray(inp["L1_ffn_b1"], f).reshape(8, 32, 128).transpose(2, 0, 1).reshape(128, 256))
    h["b2"] = np.ascontiguousarray(np.asarray(inp["L1_ffn_b2"], f))
    h["pe_fm"] = np.ascontiguousarray(_pe_table(T, 128).T)
    cz = np.zeros((128, 128), np.float32)
    cz[np.triu_indices(128, 1)] = NEG
    h["causal"] = cz
    h["causalT"] = np.ascontiguousarray(cz.T)
    h["ident_f"] = np.eye(128, dtype=np.float32)
    h["ident_b"] = np.eye(128, dtype=BFDT)
    h["ones_b"] = np.ones((1, 128), dtype=BFDT)
    return h


def _build(stage):
    nc = bass.Bass("TRN2", target_bir_lowering=False)
    dr = {}

    def dt(name, shape, dtype=F32, kind="ExternalInput"):
        dr[name] = nc.dram_tensor(name, list(shape), dtype, kind=kind)
        return dr[name]

    dt("x_in", (T, 1024))
    dt("y", (T, 1024), kind="ExternalOutput")
    dt("dbg", (128, 16384), BF16, kind="ExternalOutput")
    dt("dbgf", (128, 512), kind="ExternalOutput")
    dt("wqkv0", (8, 128, 3072)); dt("wo0", (128, 8192))
    dt("bq0", (128, 8)); dt("bk0", (128, 8)); dt("bv0", (128, 8)); dt("bo0", (1, 1024))
    dt("gate9T", (128, 72))
    dt("egu_blk", (10, 11, 128, 2048)); dt("down_blk", (10, 2, 128, 11 * 512))
    dt("wq1_blk", (4, 128, 2048)); dt("wkv1", (128, 2048)); dt("wo1", (128, 8192))
    dt("bq1", (128, 8)); dt("bkv1", (128, 2))
    dt("wgredT", (128, 128)); dt("wg", (8, 16))
    dt("w1_blk", (8, 16, 128, 2048)); dt("w2_blk", (8, 8, 128, 4096))
    dr["w1_bf"] = nc.dram_tensor("w1_bf", [8, 16, 128, 2048], BF16)
    dr["w2_bf"] = nc.dram_tensor("w2_bf", [8, 8, 128, 4096], BF16)
    dt("b1", (128, 256)); dt("b2", (8, 1024))
    dt("pe_fm", (128, 2048)); dt("causal", (128, 128)); dt("causalT", (128, 128))
    dt("ident_f", (128, 128)); dt("ident_b", (128, 128), BF16); dt("ones_b", (1, 128), BF16)

    with contextlib.ExitStack() as ctx:
        def sb(name, shape, dtype):
            return ctx.enter_context(nc.sbuf_tensor(name, list(shape), dtype))

        x_sb = sb("x_sb", (128, NT * 1024), F32)
        xn_t = sb("xn_t", (128, 1024), F32)
        fm32 = sb("fm32", (128, 1024), F32)
        rt_all = sb("rt_all", (128, 144), F32)
        xr_all = sb("xr_all", (128, 256), F32)
        scr = sb("scr", (128, 64), F32)
        stA = sb("stA", (128, 16), F32)
        stB = sb("stB", (128, 16), F32)
        stM = sb("stM", (128, 16), F32)
        stV = sb("stV", (128, 16), F32)
        stR = sb("stR", (128, 16), F32)
        st1 = sb("st1", (128, 16), F32)
        smx = sb("smx", (128, 32), F32)
        w0_all = sb("w0_all", (128, 128), F32)
        w1_all = sb("w1_all", (128, 128), F32)
        sig_all = sb("sig_all", (128, 16), F32)
        pe_s = sb("pe_s", (128, 2048), F32)
        g9_s = sb("g9_s", (128, 72), F32)
        wgr_s = sb("wgr_s", (128, 128), F32)
        wg_s = sb("wg_s", (8, 16), F32)
        wgu_s = sb("wgu_s", (8, 16), F32)
        wguT_s = sb("wguT_s", (16, 8), F32)
        n8_s = sb("n8_s", (8, 4), F32)
        xrT_b = sb("xrT_b", (16, 256), F32)
        wtT_b = sb("wtT_b", (8, 256), F32)
        b2_s = sb("b2_s", (8, 1024), F32)
        b1_s = sb("b1_s", (128, 256), F32)
        bq0_s = sb("bq0_s", (128, 8), F32)
        bk0_s = sb("bk0_s", (128, 8), F32)
        bv0_s = sb("bv0_s", (128, 8), F32)
        bq1_s = sb("bq1_s", (128, 8), F32)
        bkv1_s = sb("bkv1_s", (128, 2), F32)
        causal_s = sb("causal_s", (128, 128), F32)
        causalT_s = sb("causalT_s", (128, 128), F32)
        c30_s = sb("c30_s", (128, 1), F32)
        idf_s = sb("idf_s", (128, 128), F32)
        xn_fm = sb("xn_fm", (128, KC * T), BF16)
        idb_s = sb("idb_s", (128, 128), BF16)
        ones_s = sb("ones_s", (1, 128), BF16)
        probs = sb("probs", (128, 2 * T), BF16)
        pT = sb("pT", (128, T), BF16)
        pT1 = sb("pT1", (128, T), BF16)
        diag = sb("diag", (128, 2 * 128), BF16)
        ofm = sb("ofm", (128, 2 * 512), BF16)
        sgb = sb("sgb", (128, 2 * 512), BF16)
        bo0_b = sb("bo0_b", (1, 1024), BF16)
        pool = sb("pool", (128, 29952), BF16)

        psA = ctx.enter_context(nc.psum_tensor("psA", [128, 2048], F32))
        psB = ctx.enter_context(nc.psum_tensor("psB", [128, 2048], F32))
        psum = [psA[:, i * 512:(i + 1) * 512] for i in range(4)] + \
               [psB[:, i * 512:(i + 1) * 512] for i in range(4)]

        SD = _Sem(ctx.enter_context(nc.semaphore("SD")), "SD")
        GD = _Sem(ctx.enter_context(nc.semaphore("GD")), "GD")
        STc = _Sem(ctx.enter_context(nc.semaphore("ST")), "ST")
        SV = _Sem(ctx.enter_context(nc.semaphore("SV")), "SV")
        SS = _Sem(ctx.enter_context(nc.semaphore("SS")), "SS")

        P = _P()
        B = _Banks(P, psum)
        G = {"pT": None, "probs": [None, None], "diag": [None, None]}
        PC = {}

        def dma_g(dst_ap, src_ap, pre=()):
            P.waits("gpsimd", pre)
            return P.do("gpsimd", _fdma(dst_ap, src_ap), GD, 16)

        def dma_s(dst_ap, src_ap, pre=()):
            P.waits("sync", pre)
            return P.do("sync", _fdma(dst_ap, src_ap), SD, 16)

        ev_x = dma_s(x_sb.ap().rearrange("p (t f) -> p t f", t=NT),
                     dr["x_in"].ap().rearrange("(t p) f -> p t f", p=128))
        ev_c = None
        for nm, tile in (("pe_fm", pe_s), ("gate9T", g9_s), ("wgredT", wgr_s),
                         ("wg", wg_s), ("b2", b2_s), ("b1", b1_s),
                         ("bq0", bq0_s), ("bk0", bk0_s), ("bv0", bv0_s),
                         ("bq1", bq1_s), ("bkv1", bkv1_s), ("causal", causal_s),
                         ("causalT", causalT_s),
                         ("ident_f", idf_s), ("ident_b", idb_s), ("ones_b", ones_s)):
            ev_c = dma_s(tile[:, :], dr[nm].ap())
        ev_consts = ev_c
        ev_bo = dma_s(fm32[0:1, 0:1024], dr["bo0"].ap())
        P.wait("vector", *ev_consts)
        P.wait("vector", *ev_bo)
        ev_bo_b = P.do("vector", _fcp(bo0_b[:, :], fm32[0:1, 0:1024]), SV)
        ev_c30 = P.do("vector", lambda E: E.memset(c30_s[:, :], -30.0), SV)

        scale0 = 1.0 / 8.0
        scale1 = 1.0 / math.sqrt(128.0)

        def ln_phase(b):
            for e in ("vector", "scalar", "tensor"):
                P.wait(e, *ev_x)
                P.wait(e, *ev_consts)
            P.wait("scalar", SV, SV.n)
            for t in range(NT):
                xt = x_sb[:, t * 1024:(t + 1) * 1024]
                P.do("vector", _fred(stA[:, t:t + 1], xt, ALU.add), SV)
                P.do("scalar", _fact(psA[:, 0:1024], xt, AF.Square, acc=stB[:, t:t + 1]), SS)
            ev_sq = (SS, SS.n)
            P.waits("vector", [ev_sq])
            P.do("vector", _fts(stM[:, :], stA[:, :], 1.0 / 1024.0, ALU.mult))
            P.do("vector", _ftt(st1[:, :], stM[:, :], stM[:, :], ALU.mult))
            P.do("vector", _fts(stV[:, :], stB[:, :], 1.0 / 1024.0, ALU.mult))
            P.do("vector", _ftt(stV[:, :], stV[:, :], st1[:, :], ALU.subtract))
            ev_v = P.do("vector", _fts(stV[:, :], stV[:, :], EPS_LN, ALU.add), SV)
            if b == 1:
                P.do("vector", _ftt(wgu_s[:, :], wg_s[:, :], wg_s[:, :], ALU.mult))
                ev_n2 = P.do("vector", _fred(n8_s[:, 0:1], wgu_s[:, :], ALU.add), SV)
                P.wait("scalar", *ev_n2)
                ev_nrm = P.do("scalar", _fact(n8_s[:, 1:2], n8_s[:, 0:1], AF.Sqrt), SS)
            P.wait("scalar", *ev_v)
            ev_sd = P.do("scalar", _fact(stR[:, :], stV[:, :], AF.Sqrt), SS)
            P.wait("vector", *ev_sd)
            P.do("vector", _frecip(stR[:, :], stR[:, :]))
            P.do("vector", _ftt(st1[:, :], stR[:, :], stR[:, :], ALU.mult))
            P.do("vector", _ftt(st1[:, :], st1[:, :], stV[:, :], ALU.mult))
            P.do("vector", _fts(st1[:, :], st1[:, :], -0.5, ALU.mult, 1.5, ALU.add))
            ev_r = P.do("vector", _ftt(stR[:, :], stR[:, :], st1[:, :], ALU.mult), SV)
            if b == 1:
                P.waits("vector", [ev_nrm])
                for _ in range(2):
                    P.do("vector", _frecip(n8_s[:, 3:4], n8_s[:, 1:2]))
                    P.do("vector", _ftt(n8_s[:, 3:4], n8_s[:, 3:4], n8_s[:, 0:1], ALU.mult))
                    P.do("vector", _ftt(n8_s[:, 1:2], n8_s[:, 1:2], n8_s[:, 3:4], ALU.add))
                    P.do("vector", _fts(n8_s[:, 1:2], n8_s[:, 1:2], 0.5, ALU.mult))
                P.do("vector", _fts(n8_s[:, 1:2], n8_s[:, 1:2], 1.0e-4, ALU.max))
                P.do("vector", _frecip(n8_s[:, 2:3], n8_s[:, 1:2]))
                ev_wgu = P.do("vector", _fts(wgu_s[:, :], wg_s[:, :], n8_s[:, 2:3], ALU.mult), SV)
                P.wait("tensor", *ev_wgu)
                bk = B.acq("tensor", 6)
                ev_t = P.do("tensor", _ftr(bk[0:16, 0:8], wgu_s[:, :], idf_s[0:8, 0:8]), STc)
                P.wait("vector", *ev_t)
                ev_wguT = P.do("vector", _fcp(wguT_s[:, :], bk[0:16, 0:8]), SV)
                B.rel(6, ev_wguT)
                P.wait("tensor", *ev_wguT)
            xn_free = [None, None]
            fm_free = [None, None]
            scr_guard = None
            rt_evts = []
            for t in range(NT):
                tb = 0
                xt = x_sb[:, t * 1024:(t + 1) * 1024]
                P.waits("vector", [ev_r, xn_free[tb]])
                ev_xn = P.do("vector", _fts(xn_t[:, tb * 1024:(tb + 1) * 1024], xt,
                                            stM[:, t:t + 1], ALU.subtract,
                                            stR[:, t:t + 1], ALU.mult), SV)
                rb = B.acq("tensor", 7)
                rhs_r = g9_s if b == 0 else wgr_s
                rw = 9 if b == 0 else 16
                B.acq("tensor", 0)
                B.acq("tensor", 1)
                P.wait("tensor", *ev_xn)
                ev_tr = None
                for c in range(KC):
                    ev_tr = P.do("tensor",
                                 _ftr(psA[:, c * 128:(c + 1) * 128],
                                      xn_t[:, tb * 1024 + c * 128: tb * 1024 + (c + 1) * 128],
                                      idf_s[:, :]), STc if c == KC - 1 else None)
                P.wait("vector", *ev_tr)
                if fm_free[0] is not None:
                    P.wait("vector", *fm_free[0])
                xfm_dst = xn_fm.ap().rearrange("p (c tt) -> p c tt", c=KC)[:, :, t * 128:(t + 1) * 128]
                psrc = psA.ap().rearrange("p (c tt) -> p c tt", c=16)[:, 0:KC, :]
                ev_fm = P.dob("vector",
                              [_fcp(xfm_dst, psrc),
                               _fcp(fm32[:, :], psA[:, 0:1024])], SV)
                B.rel(0, ev_fm)
                B.rel(1, ev_fm)
                P.wait("tensor", *ev_fm)
                ev_mm = None
                for c in range(KC):
                    ev_mm = P.do("tensor", _fmm(rb[:, 0:rw], fm32[:, c * 128:(c + 1) * 128],
                                                rhs_r[:, c * rw:(c + 1) * rw],
                                                c == 0, c == KC - 1),
                                 STc if c == KC - 1 else None)
                fm_free[0] = ev_mm
                xn_free[tb] = ev_mm
                P.wait("vector", *ev_mm)
                if b == 0:
                    ev_rt = P.do("vector", _fcp(rt_all[:, t * 9:(t + 1) * 9], rb[:, 0:9]), SV)
                else:
                    ev_rt = P.do("vector", _fcp(xr_all[:, t * 16:(t + 1) * 16], rb[:, 0:16]), SV)
                B.rel(7, ev_rt)
                rt_evts.append(ev_rt)
                if b == 0:
                    P.wait("scalar", *ev_rt)
                    if scr_guard is not None:
                        P.wait("scalar", *scr_guard)
                    ev_e = P.do("scalar", _fact(scr[:, 8:16], rt_all[:, t * 9:t * 9 + 8],
                                                AF.Exp, acc=scr[:, 0:1]), SS)
                    P.wait("vector", *ev_e)
                    P.do("vector", _frecip(scr[:, 1:2], scr[:, 0:1]))
                    P.do("vector", _fred(scr[:, 2:3], scr[:, 8:16], ALU.max))
                    P.do("vector", _fts(scr[:, 16:24], scr[:, 8:16], scr[:, 2:3], ALU.is_equal))
                    P.do("vector", _fstt(scr[:, 24:32], scr[:, 16:24], NEG, scr[:, 8:16],
                                         ALU.mult, ALU.add))
                    P.do("vector", _fred(scr[:, 3:4], scr[:, 24:32], ALU.max))
                    P.do("vector", _fts(scr[:, 16:24], scr[:, 8:16], scr[:, 3:4], ALU.is_ge))
                    P.do("vector", _ftt(scr[:, 24:32], scr[:, 8:16], scr[:, 16:24], ALU.mult))
                    scr_guard = P.do("vector", _fts(w0_all[:, t * 8:(t + 1) * 8], scr[:, 24:32],
                                                    scr[:, 1:2], ALU.mult), SV)
                else:
                    xrt = t % 2
                    bk2 = B.acq("tensor", 4 + xrt)
                    P.wait("tensor", *ev_rt)
                    ev_t2 = P.do("tensor", _ftr(bk2[0:16, 0:128],
                                                xr_all[:, t * 16:(t + 1) * 16], idf_s[:, :]), STc)
                    P.wait("vector", *ev_t2)
                    ev_xrT = P.do("vector", _fcp(xrT_b[:, xrt * 128:(xrt + 1) * 128],
                                                 bk2[0:16, 0:128]), SV)
                    B.rel(4 + xrt, ev_xrT)
                    P.wait("tensor", *ev_xrT)
                    bk3 = B.acq("tensor", 6)
                    ev_lg = P.do("tensor", _fmm(bk3[:, 0:8], xrT_b[:, xrt * 128:(xrt + 1) * 128],
                                                wguT_s[:, :], True, True), STc)
                    P.wait("vector", *ev_lg)
                    ev_lgs = P.do("vector", _fcp(scr[:, 8:16], bk3[:, 0:8]), SV)
                    B.rel(6, ev_lgs)
                    P.wait("scalar", *ev_lgs)
                    if scr_guard is not None:
                        P.wait("scalar", *scr_guard)
                    ev_e = P.do("scalar", _fact(scr[:, 24:32], scr[:, 8:16],
                                                AF.Exp, acc=scr[:, 0:1]), SS)
                    P.wait("vector", *ev_e)
                    P.do("vector", _frecip(scr[:, 1:2], scr[:, 0:1]))
                    P.do("vector", _fred(scr[:, 2:3], scr[:, 24:32], ALU.max))
                    P.do("vector", _fts(scr[:, 16:24], scr[:, 24:32], scr[:, 2:3], ALU.is_equal))
                    P.do("vector", _ftt(scr[:, 3:4], scr[:, 2:3], scr[:, 1:2], ALU.mult))
                    ev_w1 = P.do("vector", _fts(w1_all[:, t * 8:(t + 1) * 8], scr[:, 16:24],
                                                scr[:, 3:4], ALU.mult), SV)
                    scr_guard = ev_w1
                    bk4 = B.acq("tensor", 4 + xrt)
                    P.wait("tensor", *ev_w1)
                    ev_t3 = P.do("tensor", _ftr(bk4[0:8, 0:128],
                                                w1_all[:, t * 8:(t + 1) * 8], idf_s[:, :]), STc)
                    P.wait("vector", *ev_t3)
                    ev_wtT = P.do("vector", _fcp(wtT_b[:, xrt * 128:(xrt + 1) * 128],
                                                 bk4[0:8, 0:128]), SV)
                    B.rel(4 + xrt, ev_wtT)
                    P.wait("tensor", *ev_wtT)
                    for half in range(2):
                        bk5 = B.acq("tensor", 2 + half)
                        ev_b2 = P.do("tensor", _fmm(bk5[:, 0:512],
                                                    wtT_b[:, xrt * 128:(xrt + 1) * 128],
                                                    b2_s[:, half * 512:(half + 1) * 512],
                                                    True, True), STc)
                        P.wait("vector", *ev_b2)
                        xsl = x_sb[:, t * 1024 + half * 512: t * 1024 + half * 512 + 512]
                        ev_add = P.do("vector", _ftt(xsl, xsl, bk5[:, 0:512], ALU.add), SV)
                        B.rel(2 + half, ev_add)
                if stage == 1:
                    P.do("vector", _fcp(xt, xn_t[:, tb * 1024:(tb + 1) * 1024]), SV)
            if b == 1 and _DBG == "ln1":
                P.wait("sync", SV, SV.n)
                dma_s(dr["dbgf"].ap()[:, 0:256], xr_all[:, :])
                dma_s(dr["dbgf"].ap()[:, 256:384], w1_all[:, :])
                dma_s(dr["dbgf"].ap()[0:16, 384:392], wguT_s[:, :])
                dma_s(dr["dbgf"].ap()[0:8, 392:396], n8_s[:, :])
                ev_dd = dma_s(dr["dbgf"].ap()[0:8, 400:416], wgu_s[:, :])
                P.wait("sync", *ev_dd)
            if b == 0:
                P.waits("scalar", rt_evts + [scr_guard])
                rt3 = rt_all.ap().rearrange("p (t n) -> p t n", n=9)
                ev_sig = P.do("scalar", _fact(sig_all[:, :], rt3[:, :, 8:9], AF.Sigmoid), SS)
                return ev_sig
            return (SV, SV.n)

        def l0_attn():
            qp, kp = pool[:, 0:2048], pool[:, 2048:4096]
            vtm = pool[:, 4096:6176]
            wbuf = [pool[:, 6176:9248], pool[:, 9248:12320]]
            aout = pool[:, 12320:28704]
            vfm = probs
            phase_guard = [(STc, STc.n), (SV, SV.n), (SS, SS.n)]
            P.wait("tensor", SV, SV.n)
            ev_w = [dma_g(wbuf[pr][:, :], dr["wqkv0"].ap()[pr], pre=phase_guard)
                    for pr in range(2)]
            wslot_free = [None, None]
            for pr in range(8):
                wb = wbuf[pr % 2]
                P.wait("tensor", *ev_w[pr])
                last_mm = None
                for wi, (dst, bias) in enumerate(((qp, bq0_s), (kp, bk0_s), (vfm, bv0_s))):
                    if wi == 2:
                        P.waits("vector", [G["probs"][0], G["probs"][1]])
                    ev = None
                    for tg in range(4):
                        bk = B.acq("tensor", tg)
                        for c in range(KC):
                            ev = P.do("tensor",
                                      _fmm(bk[:, 0:512],
                                           wb[:, wi * 1024 + c * 128:wi * 1024 + (c + 1) * 128],
                                           xn_fm[:, c * T + tg * 512: c * T + (tg + 1) * 512],
                                           c == 0, c == KC - 1),
                                      STc if c == KC - 1 else None)
                    last_mm = ev
                    P.wait("vector", *ev)
                    fns = []
                    for tg in range(4):
                        if wi == 0:
                            fns.append(_fts(dst[:, tg * 512:(tg + 1) * 512],
                                            psum[tg][:, 0:512],
                                            bias[:, pr:pr + 1], ALU.add, scale0, ALU.mult))
                        else:
                            fns.append(_fts(dst[:, tg * 512:(tg + 1) * 512],
                                            psum[tg][:, 0:512],
                                            bias[:, pr:pr + 1], ALU.add))
                    ev_ev = P.dob("vector", fns, SV)
                    for tg in range(4):
                        B.rel(tg, ev_ev)
                wslot_free[pr % 2] = last_mm
                if pr + 2 < 8:
                    ev_w.append(dma_g(wbuf[pr % 2][:, :], dr["wqkv0"].ap()[pr + 2],
                                      pre=[wslot_free[pr % 2]]))
                if stage >= 6:
                    PC[("w1", pr)] = dma_g(dr["w1_bf"].ap()[pr], dr["w1_blk"].ap()[pr])
                    PC[("w2", pr)] = dma_g(dr["w2_bf"].ap()[pr], dr["w2_blk"].ap()[pr])
                ev_vt = None
                for j in range(NT):
                    bk_i = 4 + (j % 2)
                    bk = B.acq("tensor", bk_i)
                    P.wait("tensor", *ev_ev)
                    ev_tr = P.do("tensor", _fmm(bk[:, 0:128], vfm[:, j * 128:(j + 1) * 128],
                                                idb_s[:, :], True, True), STc)
                    P.wait("vector", *ev_tr)
                    dst3 = vtm.rearrange("p (j c) -> p j c", c=130)[:, j:j + 1, :]
                    dst3 = dst3.rearrange("p j (a b) -> p (j a) b", b=65)[:, :, 0:64]
                    src3 = bk[:, 0:128].rearrange("p (a b) -> p a b", a=2)
                    ev_vt = P.do("vector", _fcp(dst3, src3), SV)
                    B.rel(bk_i, ev_vt)
                ev_one = P.do("vector",
                              lambda E, ap=vtm.rearrange("p (j c) -> p j c", c=130)[:, :, 64:65]:
                              E.memset(ap, 1.0), SV)
                P.wait("tensor", *ev_one)
                def l0_sc_exp(qt, hi, hoff):
                    X, xlo = (psA, 0) if hi == 0 else (psB, 4)
                    for j in range(4):
                        B.acq("tensor", xlo + j)
                    ev_sc = None
                    for j in range(NT):
                        ev_sc = P.do("tensor",
                                     _fmm(X[:, j * 128:(j + 1) * 128],
                                          kp[hoff:hoff + 64, j * 128:(j + 1) * 128],
                                          qp[hoff:hoff + 64, qt * 128:(qt + 1) * 128],
                                          True, True),
                                     STc if j == NT - 1 else None)
                    P.wait("scalar", *ev_c30)
                    P.wait("scalar", *ev_sc)
                    P.waits("scalar", [G["probs"][hi], G.get("sum%d" % hi)])
                    ev_e = P.do("scalar",
                                _fact(probs[:, hi * T:(hi + 1) * T], X[:, :],
                                      AF.Exp, bias=c30_s[:, :], scale=1.0), SS)
                    B.rel(xlo + 2, ev_e)
                    B.rel(xlo + 3, ev_e)
                    return ev_e

                def l0_rest(qt, hi, hoff, ev_e):
                    X, xlo = (psA, 0) if hi == 0 else (psB, 4)
                    vbase = 0 if hoff == 0 else 64
                    scol = 64 if hoff == 0 else 0
                    v0 = 0 if hoff == 0 else 1
                    P.wait("tensor", *ev_e)
                    ev_pv = None
                    for j in range(NT):
                        ev_pv = P.do("tensor",
                                     _fmm(X[:, 0:65],
                                          probs[:, hi * T + j * 128: hi * T + (j + 1) * 128],
                                          vtm[:, j * 130 + vbase: j * 130 + vbase + 65],
                                          j == 0, j == NT - 1),
                                     STc if j == NT - 1 else None)
                    G["probs"][hi] = ev_pv
                    P.wait("vector", *ev_pv)
                    P.waits("vector", [G["diag"][hi]])
                    P.do("vector", _frecip(smx[:, 2 + hi:3 + hi], X[:, scol:scol + 1]))
                    ev_dg = P.do("vector",
                                 _fts(diag[:, hi * 128:(hi + 1) * 128], idb_s[:, :],
                                      smx[:, 2 + hi:3 + hi], ALU.mult), SV)
                    G["sum%d" % hi] = ev_dg
                    ev_ob = P.do("vector", _fcp(ofm[:, hi * 512:hi * 512 + 64],
                                                X[:, v0:v0 + 64]), SV)
                    P.wait("tensor", *ev_dg)
                    P.wait("tensor", *ev_ob)
                    tp = None if hoff == 0 else (0, 64)
                    ev_t2 = P.do("tensor",
                                 _fmm(X[hoff:hoff + 64, 512:640],
                                      ofm[:, hi * 512:hi * 512 + 64],
                                      diag[:, hi * 128:(hi + 1) * 128],
                                      True, True, tp), STc)
                    G["diag"][hi] = ev_t2
                    P.wait("vector", *ev_t2)
                    ev_ao = P.do("vector",
                                 _fcp(aout[hoff:hoff + 64,
                                           pr * T + qt * 128: pr * T + (qt + 1) * 128],
                                      X[hoff:hoff + 64, 512:640]), SV)
                    B.rel(xlo + 0, ev_ao)
                    B.rel(xlo + 1, ev_ao)

                pend = None
                for qt in range(NT):
                    for hi, hoff in enumerate((0, 64)):
                        ev_e = l0_sc_exp(qt, hi, hoff)
                        if pend is not None:
                            l0_rest(*pend)
                        pend = (qt, hi, hoff, ev_e)
                l0_rest(*pend)
            if _DBG == "aout0":
                P.wait("sync", SV, SV.n)
                ev_d = dma_s(dr["dbg"].ap(), aout)
                P.wait("sync", *ev_d)
            ev_wo = dma_g(pool[:, 0:8192], dr["wo0"].ap(), pre=[(STc, STc.n)])
            P.wait("tensor", *ev_bo_b)
            for t in range(NT):
                ev_mm = None
                for half in range(2):
                    bk_i = (t * 2 + half) % 4
                    bk = B.acq("tensor", bk_i)
                    P.wait("tensor", *ev_wo)
                    P.do("tensor", _fmm(bk[:, 0:512], ones_s[0:1, 0:128],
                                        bo0_b[0:1, half * 512:(half + 1) * 512], True, False))
                    for c in range(KC):
                        ev_mm = P.do("tensor",
                                     _fmm(bk[:, 0:512],
                                          aout[:, c * T + t * 128: c * T + (t + 1) * 128],
                                          pool[:, c * 1024 + half * 512: c * 1024 + (half + 1) * 512],
                                          False, c == KC - 1),
                                     STc if c == KC - 1 else None)
                P.wait("vector", *ev_mm)
                fns = []
                for half in range(2):
                    xsl = x_sb[:, t * 1024 + half * 512: t * 1024 + half * 512 + 512]
                    fns.append(_ftt(xsl, xsl, psum[(t * 2 + half) % 4][:, 0:512], ALU.add))
                ev_add = P.dob("vector", fns, SV)
                for half in range(2):
                    B.rel((t * 2 + half) % 4, ev_add)

        def l0_moe(ev_sig):
            hbuf = pool[:, 0:11264]
            dtb = [pool[:, 11264:16896], pool[:, 16896:22528]]
            egub = [pool[:, 22528:24576], pool[:, 24576:26624]]
            phase_guard = [(STc, STc.n), (SV, SV.n), (SS, SS.n)]
            P.wait("tensor", SV, SV.n)
            P.waits("vector", [ev_sig])
            egu_free = [None, None]
            dt_free = [None, None]
            sg_free = [None, None]
            gslot = 0
            dslot = 0
            for e in range(10):
                for th in range(2):
                    h_last = None
                    for ic in range(11):
                        sl = gslot % 2
                        pre = list(phase_guard) + [egu_free[sl]]
                        ev_g = dma_g(egub[sl][:, :], dr["egu_blk"].ap()[e, ic], pre=pre)
                        gslot += 1
                        P.wait("tensor", *ev_g)
                        for tg in range(2):
                            pi = ((ic * 2 + tg) % 2) * 2
                            bkg = B.acq("tensor", pi)
                            for c in range(KC):
                                P.do("tensor",
                                     _fmm(bkg[:, 0:512], egub[sl][:, c * 128:(c + 1) * 128],
                                          xn_fm[:, c * T + th * 1024 + tg * 512:
                                                c * T + th * 1024 + (tg + 1) * 512],
                                          c == 0, c == KC - 1))
                            bku = B.acq("tensor", pi + 1)
                            ev_u = None
                            for c in range(KC):
                                ev_u = P.do("tensor",
                                            _fmm(bku[:, 0:512],
                                                 egub[sl][:, 1024 + c * 128:1024 + (c + 1) * 128],
                                                 xn_fm[:, c * T + th * 1024 + tg * 512:
                                                       c * T + th * 1024 + (tg + 1) * 512],
                                                 c == 0, c == KC - 1),
                                            STc if c == KC - 1 else None)
                            egu_free[sl] = ev_u
                            P.wait("scalar", *ev_u)
                            sgs = (ic * 2 + tg) % 2
                            P.waits("scalar", [sg_free[sgs]])
                            ev_sg = P.do("scalar", _fact(sgb[:, sgs * 512:(sgs + 1) * 512],
                                                         bkg[:, 0:512], AF.Silu), SS)
                            B.rel(pi, ev_sg)
                            P.wait("vector", *ev_sg)
                            h_last = P.do("vector",
                                          _ftt(hbuf[:, ic * 1024 + tg * 512:
                                                    ic * 1024 + (tg + 1) * 512],
                                               sgb[:, sgs * 512:(sgs + 1) * 512], bku[:, 0:512],
                                               ALU.mult), SV)
                            sg_free[sgs] = h_last
                            B.rel(pi + 1, h_last)
                    for dh in range(2):
                        sl = dslot % 2
                        pre = list(phase_guard) + [dt_free[sl]]
                        ev_d = dma_g(dtb[sl][:, :], dr["down_blk"].ap()[e, dh], pre=pre)
                        dslot += 1
                        P.wait("tensor", *ev_d)
                        P.wait("tensor", *h_last)
                        for tp2 in range(4):
                            ev_mm = None
                            bks = []
                            for ti in range(2):
                                tile = tp2 * 2 + ti
                                bk_i = 4 + ((dh * 8 + tile) % 4)
                                bk = B.acq("tensor", bk_i)
                                bks.append(bk_i)
                                for ic in range(11):
                                    ev_mm = P.do("tensor",
                                                 _fmm(bk[:, 0:512],
                                                      hbuf[:, ic * 1024 + tile * 128:
                                                           ic * 1024 + (tile + 1) * 128],
                                                      dtb[sl][:, ic * 512:(ic + 1) * 512],
                                                      ic == 0, ic == 10),
                                                 STc if ic == 10 else None)
                            dt_free[sl] = ev_mm
                            P.wait("vector", *ev_mm)
                            fns = []
                            for ti in range(2):
                                tile = tp2 * 2 + ti
                                tgl = th * 8 + tile
                                xsl = x_sb[:, tgl * 1024 + dh * 512: tgl * 1024 + dh * 512 + 512]
                                wsc = (w0_all[:, tgl * 8 + e: tgl * 8 + e + 1] if e < 8
                                       else sig_all[:, tgl:tgl + 1])
                                fns.append(_fstt(xsl, psum[bks[ti]][:, 0:512], wsc, xsl,
                                                 ALU.mult, ALU.add))
                            ev_f = P.dob("vector", fns, SV)
                            for bk_i in bks:
                                B.rel(bk_i, ev_f)

        def l1_attn():
            aout = pool[:, 12320:28704]
            qhg = pool[:, 0:4096]
            kfm = pool[:, 4096:6144]
            vtm = pool[:, 6144:8224]
            wqb = [pool[:, 8224:10272], pool[:, 10272:12320]]
            vfm = probs
            phase_guard = [(STc, STc.n), (SV, SV.n), (SS, SS.n)]
            P.wait("tensor", SV, SV.n)
            ev_kv = dma_g(wqb[0][:, :], dr["wkv1"].ap(), pre=phase_guard)
            P.wait("tensor", *ev_kv)
            ev_ev = None
            for wi in range(2):
                if wi == 1:
                    P.waits("vector", [G["probs"][0], G["probs"][1]])
                for tg in range(4):
                    bk_i = tg % 4
                    bk = B.acq("tensor", bk_i)
                    ev_mm = None
                    for c in range(KC):
                        ev_mm = P.do("tensor",
                                     _fmm(bk[:, 0:512],
                                          wqb[0][:, c * 256 + wi * 128: c * 256 + (wi + 1) * 128],
                                          xn_fm[:, c * T + tg * 512:c * T + (tg + 1) * 512],
                                          c == 0, c == KC - 1),
                                     STc if c == KC - 1 else None)
                    P.wait("vector", *ev_mm)
                    if wi == 0:
                        ev_ev = P.do("vector",
                                     _fstt(kfm[:, tg * 512:(tg + 1) * 512], bk[:, 0:512],
                                           bkv1_s[:, 0:1], pe_s[:, tg * 512:(tg + 1) * 512],
                                           ALU.add, ALU.add), SV)
                    else:
                        ev_ev = P.do("vector",
                                     _fts(vfm[:, tg * 512:(tg + 1) * 512], bk[:, 0:512],
                                          bkv1_s[:, 1:2], ALU.add), SV)
                    B.rel(bk_i, ev_ev)
            ev_vt = None
            for j in range(NT):
                bk_i = 4 + (j % 2)
                bk = B.acq("tensor", bk_i)
                P.wait("tensor", *ev_ev)
                ev_tr = P.do("tensor", _fmm(bk[:, 0:128], vfm[:, j * 128:(j + 1) * 128],
                                            idb_s[:, :], True, True), STc)
                P.wait("vector", *ev_tr)
                ev_vt = P.do("vector", _fcp(vtm[:, j * 130: j * 130 + 128], bk[:, 0:128]), SV)
                B.rel(bk_i, ev_vt)
            ev_one = P.do("vector",
                          lambda E, ap=vtm.rearrange("p (j c) -> p j c", c=130)[:, :, 128:129]:
                          E.memset(ap, 1.0), SV)
            P.wait("tensor", *ev_one)
            kv_done = (STc, STc.n)
            wq_free = [None, None]
            ev_wq = [dma_g(wqb[hg][:, :], dr["wq1_blk"].ap()[hg],
                           pre=list(phase_guard) + [kv_done]) for hg in range(2)]
            for hg in range(4):
                wb = wqb[hg % 2]
                P.wait("tensor", *ev_wq[hg])
                last_mm = None
                qev = None
                for h2 in range(2):
                    hglob = hg * 2 + h2
                    for tg in range(4):
                        bk_i = tg % 4
                        bk = B.acq("tensor", bk_i)
                        ev_mm = None
                        for c in range(KC):
                            ev_mm = P.do("tensor",
                                         _fmm(bk[:, 0:512],
                                              wb[:, c * 256 + h2 * 128: c * 256 + (h2 + 1) * 128],
                                              xn_fm[:, c * T + tg * 512:c * T + (tg + 1) * 512],
                                              c == 0, c == KC - 1),
                                         STc if c == KC - 1 else None)
                        last_mm = ev_mm
                        P.wait("vector", *ev_mm)
                        qev = P.do("vector",
                                   _fstt(qhg[:, h2 * T + tg * 512: h2 * T + (tg + 1) * 512],
                                         bk[:, 0:512], bq1_s[:, hglob:hglob + 1],
                                         pe_s[:, tg * 512:(tg + 1) * 512],
                                         ALU.add, ALU.add), SV)
                        B.rel(bk_i, qev)
                wq_free[hg % 2] = last_mm
                if hg + 2 < 4:
                    ev_wq.append(dma_g(wqb[hg % 2][:, :], dr["wq1_blk"].ap()[hg + 2],
                                       pre=[wq_free[hg % 2]]))
                P.wait("tensor", *qev)

                def l1_sc_exp(qt, h2):
                    hi = h2
                    nk = (qt + 1) * 128
                    nch = qt + 1
                    X, xlo = (psA, 0) if hi == 0 else (psB, 4)
                    for j in range(4):
                        B.acq("tensor", xlo + j)
                    ev_sc = None
                    for j in range(nch):
                        ev_sc = P.do("tensor",
                                     _fmm(X[:, j * 128:(j + 1) * 128],
                                          kfm[:, j * 128:(j + 1) * 128],
                                          qhg[:, h2 * T + qt * 128: h2 * T + (qt + 1) * 128],
                                          True, True),
                                     STc if j == nch - 1 else None)
                    P.wait("vector", *ev_sc)
                    ev_mk = P.do("vector", _ftt(X[:, qt * 128:(qt + 1) * 128],
                                                X[:, qt * 128:(qt + 1) * 128],
                                                causalT_s[:, :], ALU.add), SV)
                    P.wait("scalar", *ev_c30)
                    P.wait("scalar", *ev_mk)
                    P.waits("scalar", [G["probs"][hi], G.get("sum1%d" % hi)])
                    ev_e = P.do("scalar",
                                _fact(probs[:, hi * T: hi * T + nk], X[:, 0:nk],
                                      AF.Exp, bias=c30_s[:, :], scale=scale1), SS)
                    B.rel(xlo + 2, ev_e)
                    B.rel(xlo + 3, ev_e)
                    return ev_e

                def l1_rest(qt, h2, hglob, ev_e):
                    hi = h2
                    nch = qt + 1
                    X, xlo = (psA, 0) if hi == 0 else (psB, 4)
                    P.wait("tensor", *ev_e)
                    ev_pv = None
                    for j in range(nch):
                        ev_pv = P.do("tensor",
                                     _fmm(X[:, 0:129],
                                          probs[:, hi * T + j * 128: hi * T + (j + 1) * 128],
                                          vtm[:, j * 130: j * 130 + 129],
                                          j == 0, j == nch - 1),
                                     STc if j == nch - 1 else None)
                    G["probs"][hi] = ev_pv
                    P.wait("vector", *ev_pv)
                    P.waits("vector", [G["diag"][hi]])
                    P.do("vector", _frecip(smx[:, 2 + hi:3 + hi], X[:, 128:129]))
                    ev_dg = P.do("vector",
                                 _fts(diag[:, hi * 128:(hi + 1) * 128], idb_s[:, :],
                                      smx[:, 2 + hi:3 + hi], ALU.mult), SV)
                    G["sum1%d" % hi] = ev_dg
                    ev_ob = P.do("vector", _fcp(ofm[:, hi * 512:hi * 512 + 128],
                                                X[:, 0:128]), SV)
                    P.wait("tensor", *ev_dg)
                    P.wait("tensor", *ev_ob)
                    ev_t2 = P.do("tensor",
                                 _fmm(X[:, 512:640],
                                      ofm[:, hi * 512:hi * 512 + 128],
                                      diag[:, hi * 128:(hi + 1) * 128],
                                      True, True), STc)
                    G["diag"][hi] = ev_t2
                    P.wait("vector", *ev_t2)
                    ev_ao = P.do("vector",
                                 _fcp(aout[:, hglob * T + qt * 128: hglob * T + (qt + 1) * 128],
                                      X[:, 512:640]), SV)
                    B.rel(xlo + 0, ev_ao)
                    B.rel(xlo + 1, ev_ao)

                pend = None
                for qt in range(NT):
                    for h2 in range(2):
                        ev_e = l1_sc_exp(qt, h2)
                        if pend is not None:
                            l1_rest(*pend)
                        pend = (qt, h2, hg * 2 + h2, ev_e)
                l1_rest(*pend)
            ev_wo = dma_g(pool[:, 0:8192], dr["wo1"].ap(), pre=[(STc, STc.n)])
            for t in range(NT):
                ev_mm = None
                for half in range(2):
                    bk_i = (t * 2 + half) % 4
                    bk = B.acq("tensor", bk_i)
                    P.wait("tensor", *ev_wo)
                    for c in range(KC):
                        ev_mm = P.do("tensor",
                                     _fmm(bk[:, 0:512],
                                          aout[:, c * T + t * 128: c * T + (t + 1) * 128],
                                          pool[:, c * 1024 + half * 512: c * 1024 + (half + 1) * 512],
                                          c == 0, c == KC - 1),
                                     STc if c == KC - 1 else None)
                P.wait("vector", *ev_mm)
                fns = []
                for half in range(2):
                    xsl = x_sb[:, t * 1024 + half * 512: t * 1024 + half * 512 + 512]
                    fns.append(_ftt(xsl, xsl, psum[(t * 2 + half) % 4][:, 0:512], ALU.add))
                ev_add = P.dob("vector", fns, SV)
                for half in range(2):
                    B.rel((t * 2 + half) % 4, ev_add)

        def l1_ffn():
            hbuf = pool[:, 0:16384]
            w1b = [pool[:, 16384:18432], pool[:, 18432:20480]]
            w2b = [pool[:, 20480:22528], pool[:, 22528:24576]]
            phase_guard = [(STc, STc.n), (SV, SV.n), (SS, SS.n)]
            w1_free = [None, None]
            w2_free = [None, None]
            ofm_free = [None, None]
            s1 = 0
            s2 = 0
            for e in range(8):
                for q4 in range(4):
                    h_last = None
                    for hgp in range(16):
                        sl = s1 % 2
                        pre = list(phase_guard) + [w1_free[sl], PC[("w1", e)]]
                        ev_w = dma_g(w1b[sl][:, :], dr["w1_bf"].ap()[e, hgp], pre=pre)
                        s1 += 1
                        P.wait("tensor", *ev_w)
                        for hcl in range(2):
                            hc = hgp * 2 + hcl
                            bk_i = hc % 2
                            bk = B.acq("tensor", bk_i)
                            ev_mm = None
                            for c in range(KC):
                                ev_mm = P.do("tensor",
                                             _fmm(bk[:, 0:512],
                                                  w1b[sl][:, hcl * 1024 + c * 128:
                                                         hcl * 1024 + (c + 1) * 128],
                                                  xn_fm[:, c * T + q4 * 512:c * T + (q4 + 1) * 512],
                                                  c == 0, c == KC - 1),
                                             STc if c == KC - 1 else None)
                            w1_free[sl] = ev_mm
                            P.wait("scalar", *ev_mm)
                            h_last = P.do("scalar",
                                          _fact(hbuf[:, hc * 512:(hc + 1) * 512], bk[:, 0:512],
                                                AF.Gelu,
                                                bias=b1_s[:, e * 32 + hc:e * 32 + hc + 1]), SS)
                            B.rel(bk_i, h_last)
                    def bphase_mm(dc):
                        nonlocal s2
                        bk_i = 2 + (dc % 2)
                        bk = B.acq("tensor", bk_i)
                        ev_mm = None
                        for hh in range(2):
                            sl = s2 % 2
                            pre = list(phase_guard) + [w2_free[sl], PC[("w2", e)]]
                            ev_w = dma_g(w2b[sl][:, :],
                                         dr["w2_bf"].ap()[e, dc, :, hh * 2048:(hh + 1) * 2048],
                                         pre=pre)
                            s2 += 1
                            P.wait("tensor", *ev_w)
                            P.wait("tensor", *h_last)
                            for hcl in range(16):
                                hc = hh * 16 + hcl
                                ev_mm = P.do("tensor",
                                             _fmm(bk[:, 0:512],
                                                  w2b[sl][:, hcl * 128:(hcl + 1) * 128],
                                                  hbuf[:, hc * 512:(hc + 1) * 512],
                                                  hc == 0, hc == 31),
                                             STc if hc == 31 else None)
                            w2_free[sl] = ev_mm
                        ob = dc % 2
                        P.wait("vector", *ev_mm)
                        P.waits("vector", [ofm_free[ob]])
                        ev_of = P.do("vector", _fcp(ofm[:, ob * 512:(ob + 1) * 512],
                                                    bk[:, 0:512]), SV)
                        B.rel(bk_i, ev_of)
                        return ev_of

                    def bphase_tail(dc, ev_of):
                        ob = dc % 2
                        bk2_i = 4 + (dc % 2)
                        bk2 = B.acq("tensor", bk2_i)
                        P.wait("tensor", *ev_of)
                        ev_tr = None
                        for k in range(4):
                            ev_tr = P.do("tensor",
                                         _fmm(bk2[:, k * 128:(k + 1) * 128],
                                              ofm[:, ob * 512 + k * 128: ob * 512 + (k + 1) * 128],
                                              idb_s[:, :], True, True),
                                         STc if k == 3 else None)
                        ofm_free[ob] = ev_tr
                        P.wait("vector", *ev_tr)
                        fns = []
                        for k in range(4):
                            tile = q4 * 4 + k
                            xsl = x_sb[:, tile * 1024 + dc * 128: tile * 1024 + (dc + 1) * 128]
                            fns.append(_fstt(xsl, bk2[:, k * 128:(k + 1) * 128],
                                             w1_all[:, tile * 8 + e: tile * 8 + e + 1], xsl,
                                             ALU.mult, ALU.add))
                        ev_f = P.dob("vector", fns, SV)
                        B.rel(bk2_i, ev_f)

                    bpend = None
                    for dc in range(KC):
                        ev_of = bphase_mm(dc)
                        if bpend is not None:
                            bphase_tail(*bpend)
                        bpend = (dc, ev_of)
                    bphase_tail(*bpend)

        ev_sig = ln_phase(0)
        if stage >= 2:
            l0_attn()
        if stage >= 3:
            l0_moe(ev_sig)
        if stage >= 4:
            ln_phase(1)
        if stage >= 5:
            l1_attn()
        if stage >= 6:
            l1_ffn()
        P.wait("sync", SV, SV.n)
        P.wait("sync", SS, SS.n)
        ev_st = dma_s(dr["y"].ap().rearrange("(t p) f -> p t f", p=128),
                      x_sb.ap().rearrange("p (t f) -> p t f", t=NT))
        P.wait("sync", *ev_st)

        blk = ctx.enter_context(nc.Block())
        P.flush(blk)
    return nc


_NC_CACHE = {}


def kernel(**inputs):
    key = (_STAGE, _DBG)
    h = _prep(inputs)
    x = np.ascontiguousarray(np.asarray(inputs["L0_attn_x"], np.float32))
    if key not in _NC_CACHE:
        _NC_CACHE[key] = _build(_STAGE)
    nc = _NC_CACHE[key]
    in_maps = []
    for c in range(NCORES):
        m = {"x_in": np.ascontiguousarray(x[c])}
        m.update(h)
        in_maps.append(m)
    res = run_bass_kernel_spmd(nc, in_maps, core_ids=list(range(NCORES)))
    if _DBG:
        kernel.dbg = np.stack([np.asarray(res.results[c]["dbg"], np.float32) for c in range(NCORES)], axis=0)
        kernel.dbgf = np.stack([res.results[c]["dbgf"] for c in range(NCORES)], axis=0)
    return np.stack([res.results[c]["y"] for c in range(NCORES)], axis=0)


# version 14 (final)
# speedup vs baseline: 1.1683x; 1.1683x over previous
import math
import contextlib
import numpy as np
import ml_dtypes
import concourse.bass as bass
import concourse.mybir as mybir
from concourse.bass_utils import run_bass_kernel_spmd

F32 = mybir.dt.float32
BF16 = mybir.dt.bfloat16
AF = mybir.ActivationFunctionType
ALU = mybir.AluOpType
AX = mybir.AxisListType
BFDT = ml_dtypes.bfloat16

NCORES = 8
T = 2048
NT = 16
KC = 8
NEG = -1.0e30
EPS_LN = 1.0e-5

_STAGE = 6
_DBG = None


def _fmm(o, l, r, st, sp, tp=None):
    if tp is None:
        return lambda E: E.matmul(o, l, r, start=st, stop=sp, skip_group_check=True)
    return lambda E: E.matmul(o, l, r, start=st, stop=sp, tile_position=tp,
                              skip_group_check=True)


def _ftr(o, i, idn):
    return lambda E: E.transpose(o, i, idn)


def _fcp(o, i):
    return lambda E: E.tensor_copy(o, i)


def _ftt(o, a, b, op):
    return lambda E: E.tensor_tensor(o, a, b, op)


def _fts(o, a, s1, op0, s2=None, op1=None):
    if s2 is None:
        return lambda E: E.tensor_scalar(o, a, s1, None, op0)
    return lambda E: E.tensor_scalar(o, a, s1, s2, op0, op1)


def _fstt(o, i0, sc, i1, op0, op1):
    return lambda E: E.scalar_tensor_tensor(o, i0, sc, i1, op0, op1)


def _fact(o, i, fn, bias=0.0, scale=1.0, acc=None):
    if acc is None:
        return lambda E: E.activation(o, i, fn, bias=bias, scale=scale)
    return lambda E: E.activation(o, i, fn, bias=bias, scale=scale, accum_out=acc)


def _fred(o, i, op, neg=False):
    return lambda E: E.tensor_reduce(o, i, AX.X, op, negate=neg)


def _frecip(o, i):
    return lambda E: E.reciprocal(o, i)


def _fdma(o, i):
    return lambda E: E.dma_start(o, i)


class _Sem:
    def __init__(self, h, name):
        self.h = h
        self.name = name
        self.n = 0


class _P:
    ENG = ("tensor", "vector", "scalar", "gpsimd", "sync")

    def __init__(self):
        self.q = {e: [] for e in self.ENG}
        self.w = {e: {} for e in self.ENG}

    def on(self, e, f):
        self.q[e].append(f)

    def wait(self, e, sem, v):
        if sem is None or v is None or v <= 0:
            return
        assert v <= sem.n, ("wait on future count", e, sem.name, v, sem.n)
        if self.w[e].get(sem.name, 0) >= v:
            return
        self.w[e][sem.name] = v
        self.on(e, lambda E, s=sem.h, vv=v: E.wait_ge(s, vv))

    def waits(self, e, evts):
        for ev in evts:
            if ev is not None:
                self.wait(e, ev[0], ev[1])

    def do(self, e, f, sem=None, k=1):
        if e in ("vector", "scalar"):
            self.on(e, f)
            if sem is None:
                self.on(e, lambda E: E.drain())
                return None
            sem.n += k
            n = sem.n
            self.on(e, lambda E, s=sem.h, kk=k: E.drain().then_inc(s, kk))
            return (sem, n)
        if sem is None:
            self.on(e, f)
            return None
        sem.n += k
        n = sem.n
        self.on(e, lambda E, ff=f, s=sem.h, kk=k: ff(E).then_inc(s, kk))
        return (sem, n)

    def dob(self, e, fns, sem=None, k=1):
        for f in fns:
            self.on(e, f)
        if sem is None:
            self.on(e, lambda E: E.drain())
            return None
        sem.n += k
        n = sem.n
        self.on(e, lambda E, s=sem.h, kk=k: E.drain().then_inc(s, kk))
        return (sem, n)

    def flush(self, blk):
        for e in self.ENG:
            fns = self.q[e]
            if fns:
                getattr(blk, e)(lambda E, fns=fns: [f(E) for f in fns])


class _Banks:
    def __init__(self, p, tiles):
        self.p = p
        self.t = tiles
        self.free = [None] * len(tiles)

    def acq(self, eng, i):
        if self.free[i] is not None:
            self.p.wait(eng, *self.free[i])
        return self.t[i]

    def rel(self, i, evt):
        self.free[i] = evt


def _blk_lhsT(w, K, M):
    kc, oc = K // 128, M // 128
    a = w.reshape(kc, 128, oc, 128).transpose(2, 1, 0, 3).reshape(oc, 128, kc * 128)
    return np.ascontiguousarray(a)


def _pe_table(Tlen, d):
    pos = np.arange(Tlen, dtype=np.float64)[:, None]
    inv = np.exp(-math.log(10000.0) * np.arange(0, d, 2, dtype=np.float64) / d)
    ang = pos * inv[None, :]
    pe = np.zeros((Tlen, d), dtype=np.float64)
    pe[:, 0::2] = np.sin(ang)
    pe[:, 1::2] = np.cos(ang)
    return pe.astype(np.float32)


def _prep(inp):
    f = np.float32
    h = {}
    stk = []
    for nm in ("L0_attn_Wq", "L0_attn_Wk", "L0_attn_Wv"):
        stk.append(np.asarray(inp[nm], f).reshape(8, 128, 8, 128))
    wqkv0 = np.stack(stk, axis=0).transpose(3, 2, 0, 1, 4).reshape(8, 128, 3 * 8 * 128)
    h["wqkv0"] = np.ascontiguousarray(wqkv0)
    h["wo0"] = np.ascontiguousarray(
        np.asarray(inp["L0_attn_Wo"], f).reshape(8, 128, 1024).transpose(1, 0, 2).reshape(128, 8192))
    h["bq0"] = np.ascontiguousarray(np.asarray(inp["L0_attn_bq"], f).reshape(8, 128).T)
    h["bk0"] = np.ascontiguousarray(np.asarray(inp["L0_attn_bk"], f).reshape(8, 128).T)
    h["bv0"] = np.ascontiguousarray(np.asarray(inp["L0_attn_bv"], f).reshape(8, 128).T)
    h["bo0"] = np.ascontiguousarray(np.asarray(inp["L0_attn_bo"], f).reshape(1, 1024))
    g9 = np.concatenate([np.asarray(inp["L0_ffn_gate_w"], f).T,
                         np.asarray(inp["L0_ffn_shared_expert_gate_w"], f).T], axis=1)
    h["gate9T"] = np.ascontiguousarray(g9.reshape(8, 128, 9).transpose(1, 0, 2).reshape(128, 72))
    egu = np.asarray(inp["L0_ffn_experts_gate_up"], f)
    edn = np.asarray(inp["L0_ffn_experts_down"], f)
    sgw = np.asarray(inp["L0_ffn_shared_gate_w"], f)
    suw = np.asarray(inp["L0_ffn_shared_up_w"], f)
    sdw = np.asarray(inp["L0_ffn_shared_down_w"], f)
    g_list, u_list, d_list = [], [], []
    for e in range(8):
        g_list.append(_blk_lhsT(egu[e][:1408].T, 1024, 1408))
        u_list.append(_blk_lhsT(egu[e][1408:].T, 1024, 1408))
        d_list.append(edn[e].T)
    for sh in range(2):
        rows = slice(sh * 1408, (sh + 1) * 1408)
        g_list.append(_blk_lhsT(sgw[rows].T, 1024, 1408))
        u_list.append(_blk_lhsT(suw[rows].T, 1024, 1408))
        d_list.append(sdw.T[rows])
    gb = np.stack(g_list)
    ub = np.stack(u_list)
    h["egu_blk"] = np.ascontiguousarray(
        np.stack([gb, ub], axis=3).reshape(10, 11, 128, 2048))
    dT = np.stack(d_list)
    h["down_blk"] = np.ascontiguousarray(
        dT.reshape(10, 11, 128, 2, 512).transpose(0, 3, 2, 1, 4).reshape(10, 2, 128, 11 * 512))
    wq1 = np.asarray(inp["L1_attn_Wq"], f)
    h["wq1_blk"] = np.ascontiguousarray(
        wq1.reshape(8, 128, 4, 256).transpose(2, 1, 0, 3).reshape(4, 128, 8 * 256))
    h["wkv1"] = np.ascontiguousarray(
        np.asarray(inp["L1_attn_Wkv"], f).reshape(8, 128, 256).transpose(1, 0, 2).reshape(128, 2048))
    h["wo1"] = np.ascontiguousarray(
        np.asarray(inp["L1_attn_Wo"], f).reshape(8, 128, 1024).transpose(1, 0, 2).reshape(128, 8192))
    h["bq1"] = np.ascontiguousarray(np.asarray(inp["L1_attn_bq"], f).reshape(8, 128).T)
    h["bkv1"] = np.ascontiguousarray(np.asarray(inp["L1_attn_bkv"], f).reshape(2, 128).T)
    h["wgredT"] = np.ascontiguousarray(
        np.asarray(inp["L1_ffn_wg_red"], f).T.reshape(8, 128, 16).transpose(1, 0, 2).reshape(128, 128))
    h["wg"] = np.ascontiguousarray(np.asarray(inp["L1_ffn_wg"], f))
    w1 = np.asarray(inp["L1_ffn_w1"], f)
    w2 = np.asarray(inp["L1_ffn_w2"], f)
    w1b = np.stack([_blk_lhsT(w1[e], 1024, 4096) for e in range(8)])
    h["w1_blk"] = np.ascontiguousarray(
        w1b.reshape(8, 16, 2, 128, 1024).transpose(0, 1, 3, 2, 4).reshape(8, 16, 128, 2048))
    w2b = np.stack([_blk_lhsT(w2[e], 4096, 1024) for e in range(8)])
    h["w2_blk"] = np.ascontiguousarray(w2b)
    h["b1"] = np.ascontiguousarray(
        np.asarray(inp["L1_ffn_b1"], f).reshape(8, 32, 128).transpose(2, 0, 1).reshape(128, 256))
    h["b2"] = np.ascontiguousarray(np.asarray(inp["L1_ffn_b2"], f))
    h["pe_fm"] = np.ascontiguousarray(_pe_table(T, 128).T)
    cz = np.zeros((128, 128), np.float32)
    cz[np.triu_indices(128, 1)] = NEG
    h["causal"] = cz
    h["causalT"] = np.ascontiguousarray(cz.T)
    h["ident_f"] = np.eye(128, dtype=np.float32)
    h["ident_b"] = np.eye(128, dtype=BFDT)
    h["ones_b"] = np.ones((1, 128), dtype=BFDT)
    return h


def _build(stage):
    nc = bass.Bass("TRN2", target_bir_lowering=False)
    dr = {}

    def dt(name, shape, dtype=F32, kind="ExternalInput"):
        dr[name] = nc.dram_tensor(name, list(shape), dtype, kind=kind)
        return dr[name]

    dt("x_in", (T, 1024))
    dt("y", (T, 1024), kind="ExternalOutput")
    dt("dbg", (128, 16384), BF16, kind="ExternalOutput")
    dt("dbgf", (128, 512), kind="ExternalOutput")
    dt("wqkv0", (8, 128, 3072)); dt("wo0", (128, 8192))
    dt("bq0", (128, 8)); dt("bk0", (128, 8)); dt("bv0", (128, 8)); dt("bo0", (1, 1024))
    dt("gate9T", (128, 72))
    dt("egu_blk", (10, 11, 128, 2048)); dt("down_blk", (10, 2, 128, 11 * 512))
    dt("wq1_blk", (4, 128, 2048)); dt("wkv1", (128, 2048)); dt("wo1", (128, 8192))
    dt("bq1", (128, 8)); dt("bkv1", (128, 2))
    dt("wgredT", (128, 128)); dt("wg", (8, 16))
    dt("w1_blk", (8, 16, 128, 2048)); dt("w2_blk", (8, 8, 128, 4096))
    dr["w1_bf"] = nc.dram_tensor("w1_bf", [8, 16, 128, 2048], BF16)
    dr["w2_bf"] = nc.dram_tensor("w2_bf", [8, 8, 128, 4096], BF16)
    dt("b1", (128, 256)); dt("b2", (8, 1024))
    dt("pe_fm", (128, 2048)); dt("causal", (128, 128)); dt("causalT", (128, 128))
    dt("ident_f", (128, 128)); dt("ident_b", (128, 128), BF16); dt("ones_b", (1, 128), BF16)

    with contextlib.ExitStack() as ctx:
        def sb(name, shape, dtype):
            return ctx.enter_context(nc.sbuf_tensor(name, list(shape), dtype))

        x_sb = sb("x_sb", (128, NT * 1024), F32)
        xn_t = sb("xn_t", (128, 1024), F32)
        fm32 = sb("fm32", (128, 1024), F32)
        rt_all = sb("rt_all", (128, 144), F32)
        xr_all = sb("xr_all", (128, 256), F32)
        scr = sb("scr", (128, 64), F32)
        stA = sb("stA", (128, 16), F32)
        stB = sb("stB", (128, 16), F32)
        stM = sb("stM", (128, 16), F32)
        stV = sb("stV", (128, 16), F32)
        stR = sb("stR", (128, 16), F32)
        st1 = sb("st1", (128, 16), F32)
        smx = sb("smx", (128, 32), F32)
        w0_all = sb("w0_all", (128, 128), F32)
        w1_all = sb("w1_all", (128, 128), F32)
        sig_all = sb("sig_all", (128, 16), F32)
        pe_s = sb("pe_s", (128, 2048), F32)
        g9_s = sb("g9_s", (128, 72), F32)
        wgr_s = sb("wgr_s", (128, 128), F32)
        wg_s = sb("wg_s", (8, 16), F32)
        wgu_s = sb("wgu_s", (8, 16), F32)
        wguT_s = sb("wguT_s", (16, 8), F32)
        n8_s = sb("n8_s", (8, 4), F32)
        xrT_b = sb("xrT_b", (16, 256), F32)
        wtT_b = sb("wtT_b", (8, 256), F32)
        b2_s = sb("b2_s", (8, 1024), F32)
        b1_s = sb("b1_s", (128, 256), F32)
        bq0_s = sb("bq0_s", (128, 8), F32)
        bk0_s = sb("bk0_s", (128, 8), F32)
        bv0_s = sb("bv0_s", (128, 8), F32)
        bq1_s = sb("bq1_s", (128, 8), F32)
        bkv1_s = sb("bkv1_s", (128, 2), F32)
        causal_s = sb("causal_s", (128, 128), F32)
        causalT_s = sb("causalT_s", (128, 128), F32)
        c30_s = sb("c30_s", (128, 1), F32)
        idf_s = sb("idf_s", (128, 128), F32)
        xn_fm = sb("xn_fm", (128, KC * T), BF16)
        idb_s = sb("idb_s", (128, 128), BF16)
        ones_s = sb("ones_s", (1, 128), BF16)
        probs = sb("probs", (128, 2 * T), BF16)
        pT = sb("pT", (128, T), BF16)
        pT1 = sb("pT1", (128, T), BF16)
        diag = sb("diag", (128, 2 * 128), BF16)
        ofm = sb("ofm", (128, 2 * 512), BF16)
        sgb = sb("sgb", (128, 2 * 512), BF16)
        bo0_b = sb("bo0_b", (1, 1024), BF16)
        pool = sb("pool", (128, 29952), BF16)

        psA = ctx.enter_context(nc.psum_tensor("psA", [128, 2048], F32))
        psB = ctx.enter_context(nc.psum_tensor("psB", [128, 2048], F32))
        psum = [psA[:, i * 512:(i + 1) * 512] for i in range(4)] + \
               [psB[:, i * 512:(i + 1) * 512] for i in range(4)]

        SD = _Sem(ctx.enter_context(nc.semaphore("SD")), "SD")
        GD = _Sem(ctx.enter_context(nc.semaphore("GD")), "GD")
        STc = _Sem(ctx.enter_context(nc.semaphore("ST")), "ST")
        SV = _Sem(ctx.enter_context(nc.semaphore("SV")), "SV")
        SS = _Sem(ctx.enter_context(nc.semaphore("SS")), "SS")

        P = _P()
        B = _Banks(P, psum)
        G = {"pT": None, "probs": [None, None], "diag": [None, None]}
        PC = {}

        def dma_g(dst_ap, src_ap, pre=()):
            P.waits("gpsimd", pre)
            return P.do("gpsimd", _fdma(dst_ap, src_ap), GD, 16)

        def dma_s(dst_ap, src_ap, pre=()):
            P.waits("sync", pre)
            return P.do("sync", _fdma(dst_ap, src_ap), SD, 16)

        ev_x = dma_s(x_sb.ap().rearrange("p (t f) -> p t f", t=NT),
                     dr["x_in"].ap().rearrange("(t p) f -> p t f", p=128))
        ev_c = None
        for nm, tile in (("pe_fm", pe_s), ("gate9T", g9_s), ("wgredT", wgr_s),
                         ("wg", wg_s), ("b2", b2_s), ("b1", b1_s),
                         ("bq0", bq0_s), ("bk0", bk0_s), ("bv0", bv0_s),
                         ("bq1", bq1_s), ("bkv1", bkv1_s), ("causal", causal_s),
                         ("causalT", causalT_s),
                         ("ident_f", idf_s), ("ident_b", idb_s), ("ones_b", ones_s)):
            ev_c = dma_s(tile[:, :], dr[nm].ap())
        ev_consts = ev_c
        ev_bo = dma_s(fm32[0:1, 0:1024], dr["bo0"].ap())
        P.wait("vector", *ev_consts)
        P.wait("vector", *ev_bo)
        ev_bo_b = P.do("vector", _fcp(bo0_b[:, :], fm32[0:1, 0:1024]), SV)
        ev_c30 = P.do("vector", lambda E: E.memset(c30_s[:, :], -30.0), SV)

        scale0 = 1.0 / 8.0
        scale1 = 1.0 / math.sqrt(128.0)

        def ln_phase(b):
            for e in ("vector", "scalar", "tensor"):
                P.wait(e, *ev_x)
                P.wait(e, *ev_consts)
            P.wait("scalar", SV, SV.n)
            for t in range(NT):
                xt = x_sb[:, t * 1024:(t + 1) * 1024]
                P.do("vector", _fred(stA[:, t:t + 1], xt, ALU.add), SV)
                P.do("scalar", _fact(psA[:, 0:1024], xt, AF.Square, acc=stB[:, t:t + 1]), SS)
            ev_sq = (SS, SS.n)
            P.waits("vector", [ev_sq])
            P.do("vector", _fts(stM[:, :], stA[:, :], 1.0 / 1024.0, ALU.mult))
            P.do("vector", _ftt(st1[:, :], stM[:, :], stM[:, :], ALU.mult))
            P.do("vector", _fts(stV[:, :], stB[:, :], 1.0 / 1024.0, ALU.mult))
            P.do("vector", _ftt(stV[:, :], stV[:, :], st1[:, :], ALU.subtract))
            ev_v = P.do("vector", _fts(stV[:, :], stV[:, :], EPS_LN, ALU.add), SV)
            if b == 1:
                P.do("vector", _ftt(wgu_s[:, :], wg_s[:, :], wg_s[:, :], ALU.mult))
                ev_n2 = P.do("vector", _fred(n8_s[:, 0:1], wgu_s[:, :], ALU.add), SV)
                P.wait("scalar", *ev_n2)
                ev_nrm = P.do("scalar", _fact(n8_s[:, 1:2], n8_s[:, 0:1], AF.Sqrt), SS)
            P.wait("scalar", *ev_v)
            ev_sd = P.do("scalar", _fact(stR[:, :], stV[:, :], AF.Sqrt), SS)
            P.wait("vector", *ev_sd)
            P.do("vector", _frecip(stR[:, :], stR[:, :]))
            P.do("vector", _ftt(st1[:, :], stR[:, :], stR[:, :], ALU.mult))
            P.do("vector", _ftt(st1[:, :], st1[:, :], stV[:, :], ALU.mult))
            P.do("vector", _fts(st1[:, :], st1[:, :], -0.5, ALU.mult, 1.5, ALU.add))
            ev_r = P.do("vector", _ftt(stR[:, :], stR[:, :], st1[:, :], ALU.mult), SV)
            if b == 1:
                P.waits("vector", [ev_nrm])
                for _ in range(2):
                    P.do("vector", _frecip(n8_s[:, 3:4], n8_s[:, 1:2]))
                    P.do("vector", _ftt(n8_s[:, 3:4], n8_s[:, 3:4], n8_s[:, 0:1], ALU.mult))
                    P.do("vector", _ftt(n8_s[:, 1:2], n8_s[:, 1:2], n8_s[:, 3:4], ALU.add))
                    P.do("vector", _fts(n8_s[:, 1:2], n8_s[:, 1:2], 0.5, ALU.mult))
                P.do("vector", _fts(n8_s[:, 1:2], n8_s[:, 1:2], 1.0e-4, ALU.max))
                P.do("vector", _frecip(n8_s[:, 2:3], n8_s[:, 1:2]))
                ev_wgu = P.do("vector", _fts(wgu_s[:, :], wg_s[:, :], n8_s[:, 2:3], ALU.mult), SV)
                P.wait("tensor", *ev_wgu)
                bk = B.acq("tensor", 6)
                ev_t = P.do("tensor", _ftr(bk[0:16, 0:8], wgu_s[:, :], idf_s[0:8, 0:8]), STc)
                P.wait("vector", *ev_t)
                ev_wguT = P.do("vector", _fcp(wguT_s[:, :], bk[0:16, 0:8]), SV)
                B.rel(6, ev_wguT)
                P.wait("tensor", *ev_wguT)
            xn_free = [None, None]
            fm_free = [None, None]
            scr_guard = None
            rt_evts = []
            for t in range(NT):
                tb = 0
                xt = x_sb[:, t * 1024:(t + 1) * 1024]
                P.waits("vector", [ev_r, xn_free[tb]])
                ev_xn = P.do("vector", _fts(xn_t[:, tb * 1024:(tb + 1) * 1024], xt,
                                            stM[:, t:t + 1], ALU.subtract,
                                            stR[:, t:t + 1], ALU.mult), SV)
                rb = B.acq("tensor", 7)
                rhs_r = g9_s if b == 0 else wgr_s
                rw = 9 if b == 0 else 16
                B.acq("tensor", 0)
                B.acq("tensor", 1)
                P.wait("tensor", *ev_xn)
                ev_tr = None
                for c in range(KC):
                    ev_tr = P.do("tensor",
                                 _ftr(psA[:, c * 128:(c + 1) * 128],
                                      xn_t[:, tb * 1024 + c * 128: tb * 1024 + (c + 1) * 128],
                                      idf_s[:, :]), STc if c == KC - 1 else None)
                P.wait("vector", *ev_tr)
                if fm_free[0] is not None:
                    P.wait("vector", *fm_free[0])
                xfm_dst = xn_fm.ap().rearrange("p (c tt) -> p c tt", c=KC)[:, :, t * 128:(t + 1) * 128]
                psrc = psA.ap().rearrange("p (c tt) -> p c tt", c=16)[:, 0:KC, :]
                ev_fm = P.dob("vector",
                              [_fcp(xfm_dst, psrc),
                               _fcp(fm32[:, :], psA[:, 0:1024])], SV)
                B.rel(0, ev_fm)
                B.rel(1, ev_fm)
                P.wait("tensor", *ev_fm)
                ev_mm = None
                for c in range(KC):
                    ev_mm = P.do("tensor", _fmm(rb[:, 0:rw], fm32[:, c * 128:(c + 1) * 128],
                                                rhs_r[:, c * rw:(c + 1) * rw],
                                                c == 0, c == KC - 1),
                                 STc if c == KC - 1 else None)
                fm_free[0] = ev_mm
                xn_free[tb] = ev_mm
                P.wait("vector", *ev_mm)
                if b == 0:
                    ev_rt = P.do("vector", _fcp(rt_all[:, t * 9:(t + 1) * 9], rb[:, 0:9]), SV)
                else:
                    ev_rt = P.do("vector", _fcp(xr_all[:, t * 16:(t + 1) * 16], rb[:, 0:16]), SV)
                B.rel(7, ev_rt)
                rt_evts.append(ev_rt)
                if b == 0:
                    P.wait("scalar", *ev_rt)
                    if scr_guard is not None:
                        P.wait("scalar", *scr_guard)
                    ev_e = P.do("scalar", _fact(scr[:, 8:16], rt_all[:, t * 9:t * 9 + 8],
                                                AF.Exp, acc=scr[:, 0:1]), SS)
                    P.wait("vector", *ev_e)
                    P.do("vector", _frecip(scr[:, 1:2], scr[:, 0:1]))
                    P.do("vector", _fred(scr[:, 2:3], scr[:, 8:16], ALU.max))
                    P.do("vector", _fts(scr[:, 16:24], scr[:, 8:16], scr[:, 2:3], ALU.is_equal))
                    P.do("vector", _fstt(scr[:, 24:32], scr[:, 16:24], NEG, scr[:, 8:16],
                                         ALU.mult, ALU.add))
                    P.do("vector", _fred(scr[:, 3:4], scr[:, 24:32], ALU.max))
                    P.do("vector", _fts(scr[:, 16:24], scr[:, 8:16], scr[:, 3:4], ALU.is_ge))
                    P.do("vector", _ftt(scr[:, 24:32], scr[:, 8:16], scr[:, 16:24], ALU.mult))
                    scr_guard = P.do("vector", _fts(w0_all[:, t * 8:(t + 1) * 8], scr[:, 24:32],
                                                    scr[:, 1:2], ALU.mult), SV)
                else:
                    xrt = t % 2
                    bk2 = B.acq("tensor", 4 + xrt)
                    P.wait("tensor", *ev_rt)
                    ev_t2 = P.do("tensor", _ftr(bk2[0:16, 0:128],
                                                xr_all[:, t * 16:(t + 1) * 16], idf_s[:, :]), STc)
                    P.wait("vector", *ev_t2)
                    ev_xrT = P.do("vector", _fcp(xrT_b[:, xrt * 128:(xrt + 1) * 128],
                                                 bk2[0:16, 0:128]), SV)
                    B.rel(4 + xrt, ev_xrT)
                    P.wait("tensor", *ev_xrT)
                    bk3 = B.acq("tensor", 6)
                    ev_lg = P.do("tensor", _fmm(bk3[:, 0:8], xrT_b[:, xrt * 128:(xrt + 1) * 128],
                                                wguT_s[:, :], True, True), STc)
                    P.wait("vector", *ev_lg)
                    ev_lgs = P.do("vector", _fcp(scr[:, 8:16], bk3[:, 0:8]), SV)
                    B.rel(6, ev_lgs)
                    P.wait("scalar", *ev_lgs)
                    if scr_guard is not None:
                        P.wait("scalar", *scr_guard)
                    ev_e = P.do("scalar", _fact(scr[:, 24:32], scr[:, 8:16],
                                                AF.Exp, acc=scr[:, 0:1]), SS)
                    P.wait("vector", *ev_e)
                    P.do("vector", _frecip(scr[:, 1:2], scr[:, 0:1]))
                    P.do("vector", _fred(scr[:, 2:3], scr[:, 24:32], ALU.max))
                    P.do("vector", _fts(scr[:, 16:24], scr[:, 24:32], scr[:, 2:3], ALU.is_equal))
                    P.do("vector", _ftt(scr[:, 3:4], scr[:, 2:3], scr[:, 1:2], ALU.mult))
                    ev_w1 = P.do("vector", _fts(w1_all[:, t * 8:(t + 1) * 8], scr[:, 16:24],
                                                scr[:, 3:4], ALU.mult), SV)
                    scr_guard = ev_w1
                    bk4 = B.acq("tensor", 4 + xrt)
                    P.wait("tensor", *ev_w1)
                    ev_t3 = P.do("tensor", _ftr(bk4[0:8, 0:128],
                                                w1_all[:, t * 8:(t + 1) * 8], idf_s[:, :]), STc)
                    P.wait("vector", *ev_t3)
                    ev_wtT = P.do("vector", _fcp(wtT_b[:, xrt * 128:(xrt + 1) * 128],
                                                 bk4[0:8, 0:128]), SV)
                    B.rel(4 + xrt, ev_wtT)
                    P.wait("tensor", *ev_wtT)
                    for half in range(2):
                        bk5 = B.acq("tensor", 2 + half)
                        ev_b2 = P.do("tensor", _fmm(bk5[:, 0:512],
                                                    wtT_b[:, xrt * 128:(xrt + 1) * 128],
                                                    b2_s[:, half * 512:(half + 1) * 512],
                                                    True, True), STc)
                        P.wait("vector", *ev_b2)
                        xsl = x_sb[:, t * 1024 + half * 512: t * 1024 + half * 512 + 512]
                        ev_add = P.do("vector", _ftt(xsl, xsl, bk5[:, 0:512], ALU.add), SV)
                        B.rel(2 + half, ev_add)
                if stage == 1:
                    P.do("vector", _fcp(xt, xn_t[:, tb * 1024:(tb + 1) * 1024]), SV)
            if b == 1 and _DBG == "ln1":
                P.wait("sync", SV, SV.n)
                dma_s(dr["dbgf"].ap()[:, 0:256], xr_all[:, :])
                dma_s(dr["dbgf"].ap()[:, 256:384], w1_all[:, :])
                dma_s(dr["dbgf"].ap()[0:16, 384:392], wguT_s[:, :])
                dma_s(dr["dbgf"].ap()[0:8, 392:396], n8_s[:, :])
                ev_dd = dma_s(dr["dbgf"].ap()[0:8, 400:416], wgu_s[:, :])
                P.wait("sync", *ev_dd)
            if b == 0:
                P.waits("scalar", rt_evts + [scr_guard])
                rt3 = rt_all.ap().rearrange("p (t n) -> p t n", n=9)
                ev_sig = P.do("scalar", _fact(sig_all[:, :], rt3[:, :, 8:9], AF.Sigmoid), SS)
                return ev_sig
            return (SV, SV.n)

        def l0_attn():
            qp, kp = pool[:, 0:2048], pool[:, 2048:4096]
            vtm = pool[:, 4096:6176]
            wbuf = [pool[:, 6176:9248], pool[:, 9248:12320]]
            aout = pool[:, 12320:28704]
            vfm = probs
            phase_guard = [(STc, STc.n), (SV, SV.n), (SS, SS.n)]
            P.wait("tensor", SV, SV.n)
            ev_w = [dma_g(wbuf[pr][:, :], dr["wqkv0"].ap()[pr], pre=phase_guard)
                    for pr in range(2)]
            wslot_free = [None, None]
            for pr in range(8):
                wb = wbuf[pr % 2]
                P.wait("tensor", *ev_w[pr])
                last_mm = None
                for wi, (dst, bias) in enumerate(((qp, bq0_s), (kp, bk0_s), (vfm, bv0_s))):
                    if wi == 2:
                        P.waits("vector", [G["probs"][0], G["probs"][1]])
                    ev = None
                    for tg in range(4):
                        bk = B.acq("tensor", tg)
                        for c in range(KC):
                            ev = P.do("tensor",
                                      _fmm(bk[:, 0:512],
                                           wb[:, wi * 1024 + c * 128:wi * 1024 + (c + 1) * 128],
                                           xn_fm[:, c * T + tg * 512: c * T + (tg + 1) * 512],
                                           c == 0, c == KC - 1),
                                      STc if c == KC - 1 else None)
                    last_mm = ev
                    P.wait("vector", *ev)
                    fns = []
                    for tg in range(4):
                        if wi == 0:
                            fns.append(_fts(dst[:, tg * 512:(tg + 1) * 512],
                                            psum[tg][:, 0:512],
                                            bias[:, pr:pr + 1], ALU.add, scale0, ALU.mult))
                        else:
                            fns.append(_fts(dst[:, tg * 512:(tg + 1) * 512],
                                            psum[tg][:, 0:512],
                                            bias[:, pr:pr + 1], ALU.add))
                    ev_ev = P.dob("vector", fns, SV)
                    for tg in range(4):
                        B.rel(tg, ev_ev)
                wslot_free[pr % 2] = last_mm
                if pr + 2 < 8:
                    ev_w.append(dma_g(wbuf[pr % 2][:, :], dr["wqkv0"].ap()[pr + 2],
                                      pre=[wslot_free[pr % 2]]))
                if stage >= 6:
                    PC[("w1", pr)] = dma_g(dr["w1_bf"].ap()[pr], dr["w1_blk"].ap()[pr])
                    PC[("w2", pr)] = dma_g(dr["w2_bf"].ap()[pr], dr["w2_blk"].ap()[pr])
                ev_vt = None
                for j in range(NT):
                    bk_i = 4 + (j % 2)
                    bk = B.acq("tensor", bk_i)
                    P.wait("tensor", *ev_ev)
                    ev_tr = P.do("tensor", _fmm(bk[:, 0:128], vfm[:, j * 128:(j + 1) * 128],
                                                idb_s[:, :], True, True), STc)
                    P.wait("vector", *ev_tr)
                    dst3 = vtm.rearrange("p (j c) -> p j c", c=130)[:, j:j + 1, :]
                    dst3 = dst3.rearrange("p j (a b) -> p (j a) b", b=65)[:, :, 0:64]
                    src3 = bk[:, 0:128].rearrange("p (a b) -> p a b", a=2)
                    ev_vt = P.do("vector", _fcp(dst3, src3), SV)
                    B.rel(bk_i, ev_vt)
                ev_one = P.do("vector",
                              lambda E, ap=vtm.rearrange("p (j c) -> p j c", c=130)[:, :, 64:65]:
                              E.memset(ap, 1.0), SV)
                P.wait("tensor", *ev_one)
                def l0_sc_exp(qt, hi, hoff):
                    X, xlo = (psA, 0) if hi == 0 else (psB, 4)
                    for j in range(4):
                        B.acq("tensor", xlo + j)
                    ev_sc = None
                    for j in range(NT):
                        ev_sc = P.do("tensor",
                                     _fmm(X[:, j * 128:(j + 1) * 128],
                                          kp[hoff:hoff + 64, j * 128:(j + 1) * 128],
                                          qp[hoff:hoff + 64, qt * 128:(qt + 1) * 128],
                                          True, True),
                                     STc if j == NT - 1 else None)
                    P.wait("scalar", *ev_c30)
                    P.wait("scalar", *ev_sc)
                    P.waits("scalar", [G["probs"][hi], G.get("sum%d" % hi)])
                    ev_e = P.do("scalar",
                                _fact(probs[:, hi * T:(hi + 1) * T], X[:, :],
                                      AF.Exp, bias=c30_s[:, :], scale=1.0), SS)
                    B.rel(xlo + 2, ev_e)
                    B.rel(xlo + 3, ev_e)
                    return ev_e

                def l0_rest(qt, hi, hoff, ev_e):
                    X, xlo = (psA, 0) if hi == 0 else (psB, 4)
                    vbase = 0 if hoff == 0 else 64
                    scol = 64 if hoff == 0 else 0
                    v0 = 0 if hoff == 0 else 1
                    P.wait("tensor", *ev_e)
                    ev_pv = None
                    for j in range(NT):
                        ev_pv = P.do("tensor",
                                     _fmm(X[:, 0:65],
                                          probs[:, hi * T + j * 128: hi * T + (j + 1) * 128],
                                          vtm[:, j * 130 + vbase: j * 130 + vbase + 65],
                                          j == 0, j == NT - 1),
                                     STc if j == NT - 1 else None)
                    G["probs"][hi] = ev_pv
                    P.wait("vector", *ev_pv)
                    P.waits("vector", [G["diag"][hi]])
                    P.do("vector", _frecip(smx[:, 2 + hi:3 + hi], X[:, scol:scol + 1]))
                    ev_dg = P.do("vector",
                                 _fts(diag[:, hi * 128:(hi + 1) * 128], idb_s[:, :],
                                      smx[:, 2 + hi:3 + hi], ALU.mult), SV)
                    G["sum%d" % hi] = ev_dg
                    ev_ob = P.do("vector", _fcp(ofm[:, hi * 512:hi * 512 + 64],
                                                X[:, v0:v0 + 64]), SV)
                    P.wait("tensor", *ev_dg)
                    P.wait("tensor", *ev_ob)
                    tp = None if hoff == 0 else (0, 64)
                    ev_t2 = P.do("tensor",
                                 _fmm(X[hoff:hoff + 64, 512:640],
                                      ofm[:, hi * 512:hi * 512 + 64],
                                      diag[:, hi * 128:(hi + 1) * 128],
                                      True, True, tp), STc)
                    G["diag"][hi] = ev_t2
                    P.wait("vector", *ev_t2)
                    ev_ao = P.do("vector",
                                 _fcp(aout[hoff:hoff + 64,
                                           pr * T + qt * 128: pr * T + (qt + 1) * 128],
                                      X[hoff:hoff + 64, 512:640]), SV)
                    B.rel(xlo + 0, ev_ao)
                    B.rel(xlo + 1, ev_ao)

                pend = None
                for qt in range(NT):
                    for hi, hoff in enumerate((0, 64)):
                        ev_e = l0_sc_exp(qt, hi, hoff)
                        if pend is not None:
                            l0_rest(*pend)
                        pend = (qt, hi, hoff, ev_e)
                l0_rest(*pend)
            if _DBG == "aout0":
                P.wait("sync", SV, SV.n)
                ev_d = dma_s(dr["dbg"].ap(), aout)
                P.wait("sync", *ev_d)
            ev_wo = dma_g(pool[:, 0:8192], dr["wo0"].ap(), pre=[(STc, STc.n)])
            P.wait("tensor", *ev_bo_b)
            for t in range(NT):
                ev_mm = None
                for half in range(2):
                    bk_i = (t * 2 + half) % 4
                    bk = B.acq("tensor", bk_i)
                    P.wait("tensor", *ev_wo)
                    P.do("tensor", _fmm(bk[:, 0:512], ones_s[0:1, 0:128],
                                        bo0_b[0:1, half * 512:(half + 1) * 512], True, False))
                    for c in range(KC):
                        ev_mm = P.do("tensor",
                                     _fmm(bk[:, 0:512],
                                          aout[:, c * T + t * 128: c * T + (t + 1) * 128],
                                          pool[:, c * 1024 + half * 512: c * 1024 + (half + 1) * 512],
                                          False, c == KC - 1),
                                     STc if c == KC - 1 else None)
                P.wait("vector", *ev_mm)
                fns = []
                for half in range(2):
                    xsl = x_sb[:, t * 1024 + half * 512: t * 1024 + half * 512 + 512]
                    fns.append(_ftt(xsl, xsl, psum[(t * 2 + half) % 4][:, 0:512], ALU.add))
                ev_add = P.dob("vector", fns, SV)
                for half in range(2):
                    B.rel((t * 2 + half) % 4, ev_add)

        def l0_moe(ev_sig):
            hbuf = pool[:, 0:11264]
            dtb = [pool[:, 11264:16896], pool[:, 16896:22528]]
            egub = [pool[:, 22528:24576], pool[:, 24576:26624]]
            phase_guard = [(STc, STc.n), (SV, SV.n), (SS, SS.n)]
            P.wait("tensor", SV, SV.n)
            P.waits("vector", [ev_sig])
            egu_free = [None, None]
            dt_free = [None, None]
            sg_free = [None, None]
            gslot = 0
            dslot = 0
            for e in range(10):
                for th in range(2):
                    h_last = None
                    for ic in range(11):
                        sl = gslot % 2
                        pre = list(phase_guard) + [egu_free[sl]]
                        ev_g = dma_g(egub[sl][:, :], dr["egu_blk"].ap()[e, ic], pre=pre)
                        gslot += 1
                        P.wait("tensor", *ev_g)
                        for tg in range(2):
                            pi = ((ic * 2 + tg) % 2) * 2
                            bkg = B.acq("tensor", pi)
                            for c in range(KC):
                                P.do("tensor",
                                     _fmm(bkg[:, 0:512], egub[sl][:, c * 128:(c + 1) * 128],
                                          xn_fm[:, c * T + th * 1024 + tg * 512:
                                                c * T + th * 1024 + (tg + 1) * 512],
                                          c == 0, c == KC - 1))
                            bku = B.acq("tensor", pi + 1)
                            ev_u = None
                            for c in range(KC):
                                ev_u = P.do("tensor",
                                            _fmm(bku[:, 0:512],
                                                 egub[sl][:, 1024 + c * 128:1024 + (c + 1) * 128],
                                                 xn_fm[:, c * T + th * 1024 + tg * 512:
                                                       c * T + th * 1024 + (tg + 1) * 512],
                                                 c == 0, c == KC - 1),
                                            STc if c == KC - 1 else None)
                            egu_free[sl] = ev_u
                            P.wait("scalar", *ev_u)
                            sgs = (ic * 2 + tg) % 2
                            P.waits("scalar", [sg_free[sgs]])
                            ev_sg = P.do("scalar", _fact(sgb[:, sgs * 512:(sgs + 1) * 512],
                                                         bkg[:, 0:512], AF.Silu), SS)
                            B.rel(pi, ev_sg)
                            P.wait("vector", *ev_sg)
                            h_last = P.do("vector",
                                          _ftt(hbuf[:, ic * 1024 + tg * 512:
                                                    ic * 1024 + (tg + 1) * 512],
                                               sgb[:, sgs * 512:(sgs + 1) * 512], bku[:, 0:512],
                                               ALU.mult), SV)
                            sg_free[sgs] = h_last
                            B.rel(pi + 1, h_last)
                    for dh in range(2):
                        sl = dslot % 2
                        pre = list(phase_guard) + [dt_free[sl]]
                        ev_d = dma_g(dtb[sl][:, :], dr["down_blk"].ap()[e, dh], pre=pre)
                        dslot += 1
                        P.wait("tensor", *ev_d)
                        P.wait("tensor", *h_last)
                        for tp2 in range(4):
                            ev_mm = None
                            bks = []
                            for ti in range(2):
                                tile = tp2 * 2 + ti
                                bk_i = 4 + ((dh * 8 + tile) % 4)
                                bk = B.acq("tensor", bk_i)
                                bks.append(bk_i)
                                for ic in range(11):
                                    ev_mm = P.do("tensor",
                                                 _fmm(bk[:, 0:512],
                                                      hbuf[:, ic * 1024 + tile * 128:
                                                           ic * 1024 + (tile + 1) * 128],
                                                      dtb[sl][:, ic * 512:(ic + 1) * 512],
                                                      ic == 0, ic == 10),
                                                 STc if ic == 10 else None)
                            dt_free[sl] = ev_mm
                            P.wait("vector", *ev_mm)
                            fns = []
                            for ti in range(2):
                                tile = tp2 * 2 + ti
                                tgl = th * 8 + tile
                                xsl = x_sb[:, tgl * 1024 + dh * 512: tgl * 1024 + dh * 512 + 512]
                                wsc = (w0_all[:, tgl * 8 + e: tgl * 8 + e + 1] if e < 8
                                       else sig_all[:, tgl:tgl + 1])
                                fns.append(_fstt(xsl, psum[bks[ti]][:, 0:512], wsc, xsl,
                                                 ALU.mult, ALU.add))
                            ev_f = P.dob("vector", fns, SV)
                            for bk_i in bks:
                                B.rel(bk_i, ev_f)

        def l1_attn():
            aout = pool[:, 12320:28704]
            qhg = pool[:, 0:4096]
            kfm = pool[:, 4096:6144]
            vtm = pool[:, 6144:8224]
            wqb = [pool[:, 8224:10272], pool[:, 10272:12320]]
            vfm = probs
            phase_guard = [(STc, STc.n), (SV, SV.n), (SS, SS.n)]
            P.wait("tensor", SV, SV.n)
            ev_kv = dma_g(wqb[0][:, :], dr["wkv1"].ap(), pre=phase_guard)
            P.wait("tensor", *ev_kv)
            ev_ev = None
            for wi in range(2):
                if wi == 1:
                    P.waits("vector", [G["probs"][0], G["probs"][1]])
                for tg in range(4):
                    bk_i = tg % 4
                    bk = B.acq("tensor", bk_i)
                    ev_mm = None
                    for c in range(KC):
                        ev_mm = P.do("tensor",
                                     _fmm(bk[:, 0:512],
                                          wqb[0][:, c * 256 + wi * 128: c * 256 + (wi + 1) * 128],
                                          xn_fm[:, c * T + tg * 512:c * T + (tg + 1) * 512],
                                          c == 0, c == KC - 1),
                                     STc if c == KC - 1 else None)
                    P.wait("vector", *ev_mm)
                    if wi == 0:
                        ev_ev = P.do("vector",
                                     _fstt(kfm[:, tg * 512:(tg + 1) * 512], bk[:, 0:512],
                                           bkv1_s[:, 0:1], pe_s[:, tg * 512:(tg + 1) * 512],
                                           ALU.add, ALU.add), SV)
                    else:
                        ev_ev = P.do("vector",
                                     _fts(vfm[:, tg * 512:(tg + 1) * 512], bk[:, 0:512],
                                          bkv1_s[:, 1:2], ALU.add), SV)
                    B.rel(bk_i, ev_ev)
            ev_vt = None
            for j in range(NT):
                bk_i = 4 + (j % 2)
                bk = B.acq("tensor", bk_i)
                P.wait("tensor", *ev_ev)
                ev_tr = P.do("tensor", _fmm(bk[:, 0:128], vfm[:, j * 128:(j + 1) * 128],
                                            idb_s[:, :], True, True), STc)
                P.wait("vector", *ev_tr)
                ev_vt = P.do("vector", _fcp(vtm[:, j * 130: j * 130 + 128], bk[:, 0:128]), SV)
                B.rel(bk_i, ev_vt)
            ev_one = P.do("vector",
                          lambda E, ap=vtm.rearrange("p (j c) -> p j c", c=130)[:, :, 128:129]:
                          E.memset(ap, 1.0), SV)
            P.wait("tensor", *ev_one)
            kv_done = (STc, STc.n)
            wq_free = [None, None]
            ev_wq = [dma_g(wqb[hg][:, :], dr["wq1_blk"].ap()[hg],
                           pre=list(phase_guard) + [kv_done]) for hg in range(2)]
            for hg in range(4):
                wb = wqb[hg % 2]
                P.wait("tensor", *ev_wq[hg])
                last_mm = None
                qev = None
                for h2 in range(2):
                    hglob = hg * 2 + h2
                    for tg in range(4):
                        bk_i = tg % 4
                        bk = B.acq("tensor", bk_i)
                        ev_mm = None
                        for c in range(KC):
                            ev_mm = P.do("tensor",
                                         _fmm(bk[:, 0:512],
                                              wb[:, c * 256 + h2 * 128: c * 256 + (h2 + 1) * 128],
                                              xn_fm[:, c * T + tg * 512:c * T + (tg + 1) * 512],
                                              c == 0, c == KC - 1),
                                         STc if c == KC - 1 else None)
                        last_mm = ev_mm
                        P.wait("vector", *ev_mm)
                        qev = P.do("vector",
                                   _fstt(qhg[:, h2 * T + tg * 512: h2 * T + (tg + 1) * 512],
                                         bk[:, 0:512], bq1_s[:, hglob:hglob + 1],
                                         pe_s[:, tg * 512:(tg + 1) * 512],
                                         ALU.add, ALU.add), SV)
                        B.rel(bk_i, qev)
                wq_free[hg % 2] = last_mm
                if hg + 2 < 4:
                    ev_wq.append(dma_g(wqb[hg % 2][:, :], dr["wq1_blk"].ap()[hg + 2],
                                       pre=[wq_free[hg % 2]]))
                P.wait("tensor", *qev)

                def l1_sc_exp(qt, h2):
                    hi = h2
                    nk = (qt + 1) * 128
                    nch = qt + 1
                    X, xlo = (psA, 0) if hi == 0 else (psB, 4)
                    for j in range(4):
                        B.acq("tensor", xlo + j)
                    ev_sc = None
                    for j in range(nch):
                        ev_sc = P.do("tensor",
                                     _fmm(X[:, j * 128:(j + 1) * 128],
                                          kfm[:, j * 128:(j + 1) * 128],
                                          qhg[:, h2 * T + qt * 128: h2 * T + (qt + 1) * 128],
                                          True, True),
                                     STc if j == nch - 1 else None)
                    P.wait("vector", *ev_sc)
                    ev_mk = P.do("vector", _ftt(X[:, qt * 128:(qt + 1) * 128],
                                                X[:, qt * 128:(qt + 1) * 128],
                                                causalT_s[:, :], ALU.add), SV)
                    P.wait("scalar", *ev_c30)
                    P.wait("scalar", *ev_mk)
                    P.waits("scalar", [G["probs"][hi], G.get("sum1%d" % hi)])
                    ev_e = P.do("scalar",
                                _fact(probs[:, hi * T: hi * T + nk], X[:, 0:nk],
                                      AF.Exp, bias=c30_s[:, :], scale=scale1), SS)
                    B.rel(xlo + 2, ev_e)
                    B.rel(xlo + 3, ev_e)
                    return ev_e

                def l1_rest(qt, h2, hglob, ev_e):
                    hi = h2
                    nch = qt + 1
                    X, xlo = (psA, 0) if hi == 0 else (psB, 4)
                    P.wait("tensor", *ev_e)
                    ev_pv = None
                    for j in range(nch):
                        ev_pv = P.do("tensor",
                                     _fmm(X[:, 0:129],
                                          probs[:, hi * T + j * 128: hi * T + (j + 1) * 128],
                                          vtm[:, j * 130: j * 130 + 129],
                                          j == 0, j == nch - 1),
                                     STc if j == nch - 1 else None)
                    G["probs"][hi] = ev_pv
                    P.wait("vector", *ev_pv)
                    P.waits("vector", [G["diag"][hi]])
                    P.do("vector", _frecip(smx[:, 2 + hi:3 + hi], X[:, 128:129]))
                    ev_dg = P.do("vector",
                                 _fts(diag[:, hi * 128:(hi + 1) * 128], idb_s[:, :],
                                      smx[:, 2 + hi:3 + hi], ALU.mult), SV)
                    G["sum1%d" % hi] = ev_dg
                    ev_ob = P.do("vector", _fcp(ofm[:, hi * 512:hi * 512 + 128],
                                                X[:, 0:128]), SV)
                    P.wait("tensor", *ev_dg)
                    P.wait("tensor", *ev_ob)
                    ev_t2 = P.do("tensor",
                                 _fmm(X[:, 512:640],
                                      ofm[:, hi * 512:hi * 512 + 128],
                                      diag[:, hi * 128:(hi + 1) * 128],
                                      True, True), STc)
                    G["diag"][hi] = ev_t2
                    P.wait("vector", *ev_t2)
                    ev_ao = P.do("vector",
                                 _fcp(aout[:, hglob * T + qt * 128: hglob * T + (qt + 1) * 128],
                                      X[:, 512:640]), SV)
                    B.rel(xlo + 0, ev_ao)
                    B.rel(xlo + 1, ev_ao)

                pend = None
                for qt in range(NT):
                    for h2 in range(2):
                        ev_e = l1_sc_exp(qt, h2)
                        if pend is not None:
                            l1_rest(*pend)
                        pend = (qt, h2, hg * 2 + h2, ev_e)
                l1_rest(*pend)
            ev_wo = dma_g(pool[:, 0:8192], dr["wo1"].ap(), pre=[(STc, STc.n)])
            for t in range(NT):
                ev_mm = None
                for half in range(2):
                    bk_i = (t * 2 + half) % 4
                    bk = B.acq("tensor", bk_i)
                    P.wait("tensor", *ev_wo)
                    for c in range(KC):
                        ev_mm = P.do("tensor",
                                     _fmm(bk[:, 0:512],
                                          aout[:, c * T + t * 128: c * T + (t + 1) * 128],
                                          pool[:, c * 1024 + half * 512: c * 1024 + (half + 1) * 512],
                                          c == 0, c == KC - 1),
                                     STc if c == KC - 1 else None)
                P.wait("vector", *ev_mm)
                fns = []
                for half in range(2):
                    xsl = x_sb[:, t * 1024 + half * 512: t * 1024 + half * 512 + 512]
                    fns.append(_ftt(xsl, xsl, psum[(t * 2 + half) % 4][:, 0:512], ALU.add))
                ev_add = P.dob("vector", fns, SV)
                for half in range(2):
                    B.rel((t * 2 + half) % 4, ev_add)

        def l1_ffn():
            hbuf = pool[:, 0:16384]
            w1b = [pool[:, 16384:18432], pool[:, 18432:20480]]
            w2b = [pool[:, 20480:22528], pool[:, 22528:24576]]
            phase_guard = [(STc, STc.n), (SV, SV.n), (SS, SS.n)]
            w1_free = [None, None]
            w2_free = [None, None]
            ofm_free = [None, None]
            s1 = 0
            s2 = 0
            for e in range(8):
                for q4 in range(4):
                    h_last = None
                    for hgp in range(16):
                        sl = s1 % 2
                        pre = list(phase_guard) + [w1_free[sl], PC[("w1", e)]]
                        ev_w = dma_s(w1b[sl][:, :], dr["w1_bf"].ap()[e, hgp], pre=pre)
                        s1 += 1
                        P.wait("tensor", *ev_w)
                        for hcl in range(2):
                            hc = hgp * 2 + hcl
                            bk_i = hc % 2
                            bk = B.acq("tensor", bk_i)
                            ev_mm = None
                            for c in range(KC):
                                ev_mm = P.do("tensor",
                                             _fmm(bk[:, 0:512],
                                                  w1b[sl][:, hcl * 1024 + c * 128:
                                                         hcl * 1024 + (c + 1) * 128],
                                                  xn_fm[:, c * T + q4 * 512:c * T + (q4 + 1) * 512],
                                                  c == 0, c == KC - 1),
                                             STc if c == KC - 1 else None)
                            w1_free[sl] = ev_mm
                            P.wait("scalar", *ev_mm)
                            h_last = P.do("scalar",
                                          _fact(hbuf[:, hc * 512:(hc + 1) * 512], bk[:, 0:512],
                                                AF.Gelu,
                                                bias=b1_s[:, e * 32 + hc:e * 32 + hc + 1]), SS)
                            B.rel(bk_i, h_last)
                    def bphase_mm(dc):
                        nonlocal s2
                        bk_i = 2 + (dc % 2)
                        bk = B.acq("tensor", bk_i)
                        ev_mm = None
                        for hh in range(2):
                            sl = s2 % 2
                            pre = list(phase_guard) + [w2_free[sl], PC[("w2", e)]]
                            ev_w = dma_s(w2b[sl][:, :],
                                         dr["w2_bf"].ap()[e, dc, :, hh * 2048:(hh + 1) * 2048],
                                         pre=pre)
                            s2 += 1
                            P.wait("tensor", *ev_w)
                            P.wait("tensor", *h_last)
                            for hcl in range(16):
                                hc = hh * 16 + hcl
                                ev_mm = P.do("tensor",
                                             _fmm(bk[:, 0:512],
                                                  w2b[sl][:, hcl * 128:(hcl + 1) * 128],
                                                  hbuf[:, hc * 512:(hc + 1) * 512],
                                                  hc == 0, hc == 31),
                                             STc if hc == 31 else None)
                            w2_free[sl] = ev_mm
                        ob = dc % 2
                        P.wait("vector", *ev_mm)
                        P.waits("vector", [ofm_free[ob]])
                        ev_of = P.do("vector", _fcp(ofm[:, ob * 512:(ob + 1) * 512],
                                                    bk[:, 0:512]), SV)
                        B.rel(bk_i, ev_of)
                        return ev_of

                    def bphase_tail(dc, ev_of):
                        ob = dc % 2
                        bk2_i = 4 + (dc % 2)
                        bk2 = B.acq("tensor", bk2_i)
                        P.wait("tensor", *ev_of)
                        ev_tr = None
                        for k in range(4):
                            ev_tr = P.do("tensor",
                                         _fmm(bk2[:, k * 128:(k + 1) * 128],
                                              ofm[:, ob * 512 + k * 128: ob * 512 + (k + 1) * 128],
                                              idb_s[:, :], True, True),
                                         STc if k == 3 else None)
                        ofm_free[ob] = ev_tr
                        P.wait("vector", *ev_tr)
                        fns = []
                        for k in range(4):
                            tile = q4 * 4 + k
                            xsl = x_sb[:, tile * 1024 + dc * 128: tile * 1024 + (dc + 1) * 128]
                            fns.append(_fstt(xsl, bk2[:, k * 128:(k + 1) * 128],
                                             w1_all[:, tile * 8 + e: tile * 8 + e + 1], xsl,
                                             ALU.mult, ALU.add))
                        ev_f = P.dob("vector", fns, SV)
                        B.rel(bk2_i, ev_f)

                    bpend = None
                    for dc in range(KC):
                        ev_of = bphase_mm(dc)
                        if bpend is not None:
                            bphase_tail(*bpend)
                        bpend = (dc, ev_of)
                    bphase_tail(*bpend)

        ev_sig = ln_phase(0)
        if stage >= 2:
            l0_attn()
        if stage >= 3:
            l0_moe(ev_sig)
        if stage >= 4:
            ln_phase(1)
        if stage >= 5:
            l1_attn()
        if stage >= 6:
            l1_ffn()
        P.wait("sync", SV, SV.n)
        P.wait("sync", SS, SS.n)
        ev_st = dma_s(dr["y"].ap().rearrange("(t p) f -> p t f", p=128),
                      x_sb.ap().rearrange("p (t f) -> p t f", t=NT))
        P.wait("sync", *ev_st)

        blk = ctx.enter_context(nc.Block())
        P.flush(blk)
    return nc


_NC_CACHE = {}


def kernel(**inputs):
    key = (_STAGE, _DBG)
    h = _prep(inputs)
    x = np.ascontiguousarray(np.asarray(inputs["L0_attn_x"], np.float32))
    if key not in _NC_CACHE:
        _NC_CACHE[key] = _build(_STAGE)
    nc = _NC_CACHE[key]
    in_maps = []
    for c in range(NCORES):
        m = {"x_in": np.ascontiguousarray(x[c])}
        m.update(h)
        in_maps.append(m)
    res = run_bass_kernel_spmd(nc, in_maps, core_ids=list(range(NCORES)))
    if _DBG:
        kernel.dbg = np.stack([np.asarray(res.results[c]["dbg"], np.float32) for c in range(NCORES)], axis=0)
        kernel.dbgf = np.stack([res.results[c]["dbgf"] for c in range(NCORES)], axis=0)
    return np.stack([res.results[c]["y"] for c in range(NCORES)], axis=0)
